# version 07
# baseline (speedup 1.0000x reference)
import numpy as np
import ml_dtypes
from contextlib import ExitStack

import concourse.bass as bass
import concourse.mybir as mybir
from concourse.bass_utils import run_bass_kernel_spmd

F32 = mybir.dt.float32
BF16 = mybir.dt.bfloat16
I32 = mybir.dt.int32
AF = mybir.ActivationFunctionType
ALU = mybir.AluOpType
AX = mybir.AxisListType
BF = ml_dtypes.bfloat16

D = 2048
DT = 16
H = 8192
S = 1024
NSTRIP = 4
CH = 256
NSL = 16
TPS = 8
NW = 4
WJ = 2048
EPS_LN = 1e-5
EPS_RMS = 1e-6
EPS_Q = 1e-5
NCORE = 8

DBG = {}

CPOOL = [EPS_LN, EPS_RMS, 1.0, -0.5, 0.5, -1.0, 2.0, 0.001]



class Buf:
    __slots__ = ("name", "writer", "readers")

    def __init__(self, name):
        self.name = name
        self.writer = None
        self.readers = []


class KB:
    ENG = ("sync", "pe", "dve", "act", "gpsimd")
    NDMACH = 40

    def __init__(self, nc):
        self.nc = nc
        self.prog = {e: [] for e in self.ENG}
        self.counts = {s: 0 for s in ("pe", "dve", "act", "pool")}
        self.eng_sem = {"pe": "pe", "dve": "dve", "act": "act", "gpsimd": "pool"}
        self.wm = {}
        self.sems = {}
        self.chmap = {}
        for i in range(self.NDMACH):
            self.counts[f"dq{i}"] = 0

    def _wait(self, eng, token):
        if token is None:
            return
        sem, val = token
        if self.wm.get((eng, sem), 0) >= val:
            return
        if self.eng_sem.get(eng) == sem:
            assert self.counts[sem] >= val, (eng, sem, val, self.counts[sem])
        self.prog[eng].append(("wait", sem, val))
        self.wm[(eng, sem)] = val

    def op(self, eng, emit, reads=(), writes=(), sem=None, inc=1):
        if sem is None:
            sem = self.eng_sem[eng]
        for b in reads:
            self._wait(eng, b.writer)
        for b in writes:
            self._wait(eng, b.writer)
            for t in b.readers:
                self._wait(eng, t)
        self.counts[sem] += inc
        tok = (sem, self.counts[sem])
        self.prog[eng].append(("emit", emit, sem, inc,
                               [(b.name, b.writer) for b in reads],
                               [(b.name, b.writer, list(b.readers)) for b in writes]))
        for b in writes:
            b.writer = tok
            b.readers = []
        for b in reads:
            b.readers.append(tok)
        return tok

    def dma(self, eng, out_ap, in_ap, reads=(), writes=(), ch="misc"):
        if ch not in self.chmap:
            assert len(self.chmap) < self.NDMACH, "out of dma channels"
            self.chmap[ch] = f"dq{len(self.chmap)}"
            self.chowner = getattr(self, "chowner", {})
            self.chowner[ch] = eng
        assert self.chowner[ch] == eng, f"channel {ch} used from {eng} and {self.chowner[ch]}"
        sem = self.chmap[ch]
        self._wait(eng, (sem, self.counts[sem]))

        def emit(h, out_ap=out_ap, in_ap=in_ap):
            return h.dma_start(out=out_ap, in_=in_ap)
        return self.op(eng, emit, reads=reads, writes=writes, sem=sem, inc=16)

    def barrier_read(self, eng, bufs):
        for b in bufs:
            self._wait(eng, b.writer)

    def wait_all_dma(self, eng):
        for ch, sem in self.chmap.items():
            self.prog[eng].append(("wait", sem, self.counts[sem]))

    def replay(self, block):
        sems = self.sems

        def mk(items):
            def body(h):
                for it in items:
                    if it[0] == "wait":
                        h.wait_ge(sems[it[1]], it[2])
                    else:
                        it[1](h).then_inc(sems[it[2]], it[3])
            return body
        block.sync(mk(self.prog["sync"]))
        block.tensor(mk(self.prog["pe"]))
        block.vector(mk(self.prog["dve"]))
        block.scalar(mk(self.prog["act"]))
        block.gpsimd(mk(self.prog["gpsimd"]))



class Arena:
    def __init__(self, tensor, cols, name, min_class=512):
        self.t = tensor
        self.cols = cols
        self.name = name
        self.min_class = min_class
        self.off = 0
        self.freelist = {}
        self.live = {}

    def alloc(self, cols, tag=""):
        req = cols
        cls = self.min_class
        while cls < cols:
            cls *= 2
        cols = cls
        lst = self.freelist.get(cols)
        if lst:
            off, buf = lst.pop()
        else:
            bigger = sorted(sz for sz, l in self.freelist.items() if sz > cols and l)
            if bigger:
                sz = bigger[0]
                off, buf = self.freelist[sz].pop()
                rem = sz - cols
                o2 = off + cols
                while rem:
                    half = cols
                    while half * 2 <= rem:
                        half *= 2
                    nb = Buf(f"{self.name}:{o2}")
                    nb.writer = buf.writer
                    nb.readers = list(buf.readers)
                    self.freelist.setdefault(half, []).append((o2, nb))
                    o2 += half
                    rem -= half
            else:
                al = min(cols, 4096)
                off = (self.off + al - 1) // al * al
                for gap_off in range(self.off, off, self.min_class):
                    gb = Buf(f"{self.name}:{gap_off}")
                    self.freelist.setdefault(self.min_class, []).append((gap_off, gb))
                assert off + cols <= self.cols, \
                    f"arena {self.name} overflow: {off}+{cols}>{self.cols} ({tag})"
                self.off = off + cols
                buf = Buf(f"{self.name}:{off}")
        for lo, ln in self.live.items():
            assert off + cols <= lo or lo + ln <= off, \
                f"arena {self.name}: overlap alloc [{off},{off+cols}) with live [{lo},{lo+ln}) ({tag})"
        self.live[off] = cols
        return Region(self, off, cols, buf, req)

    def release(self, reg):
        assert self.live.pop(reg.off, None) is not None, \
            f"double free {self.name}@{reg.off}"
        off, cols, buf = reg.off, reg.cols, reg.buf
        while cols < 4096:
            buddy = off ^ cols
            lst = self.freelist.get(cols, [])
            hit = next((i for i, (o, _b) in enumerate(lst) if o == buddy), None)
            if hit is None:
                break
            bo, bb = lst.pop(hit)
            nb = Buf(f"{self.name}:{min(off, bo)}")
            nb.readers = [t for t in (buf.writer, bb.writer) if t] \
                + buf.readers + bb.readers
            buf = nb
            off = min(off, bo)
            cols *= 2
        self.freelist.setdefault(cols, []).append((off, buf))


class Region:
    __slots__ = ("arena", "off", "cols", "buf", "req", "kids")

    def __init__(self, arena, off, cols, buf, req=None):
        self.arena = arena
        self.off = off
        self.cols = cols
        self.buf = buf
        self.req = req if req is not None else cols
        self.kids = None

    def carve(self, sizes):
        regs = []
        o = self.off
        for sz in sizes:
            b = Buf(f"{self.buf.name}+{o - self.off}")
            b.writer = self.buf.writer
            b.readers = list(self.buf.readers)
            regs.append(Region(self.arena, o, sz, b))
            o += sz
        assert o <= self.off + self.cols
        self.kids = regs
        return regs

    def ap(self, c0=0, cols=None, p0=0, pn=128):
        cols = self.req - c0 if cols is None else cols
        assert c0 + cols <= self.cols, (c0, cols, self.cols)
        return self.arena.t[p0:p0 + pn, self.off + c0:self.off + c0 + cols]

    def free(self):
        if self.kids:
            for kid in self.kids:
                if kid.buf.writer:
                    self.buf.readers.append(kid.buf.writer)
                self.buf.readers.extend(kid.buf.readers)
            self.kids = None
        self.arena.release(self)



def _mixer_perm():
    cp = np.arange(D)
    dh, hh = cp // 16, cp % 16
    return hh * 128 + dh


def prep_inputs(L0_attn_x, L0_attn_norm_scale, L0_attn_conv_w, L0_attn_conv_b,
                L0_attn_in_proj_w, L0_attn_dt_bias, L0_attn_head_mix,
                L0_attn_out_proj_w, L0_ffn_c_fc, L0_ffn_c_proj):
    perm = _mixer_perm()
    x = np.asarray(L0_attn_x, np.float32)
    wint = np.asarray(L0_attn_in_proj_w, np.float32)
    wint = wint.reshape(3, D, D)[:, perm, :]
    wint = wint.reshape(3 * D, D).T
    wint = np.ascontiguousarray(
        wint.reshape(DT, 128, 48, 128).transpose(1, 2, 0, 3).reshape(128, 48 * DT * 128))
    wo = np.asarray(L0_attn_out_proj_w, np.float32)[:, perm]
    woT = wo.T
    woT = np.ascontiguousarray(
        woT.reshape(DT, 128, DT, 128).transpose(1, 2, 0, 3).reshape(128, DT * DT * 128))
    w1 = np.asarray(L0_ffn_c_fc, np.float32)
    w1 = np.ascontiguousarray(
        w1.reshape(DT, 128, 32, 256).transpose(1, 2, 0, 3).reshape(128, 32 * DT * 256))
    w2t = np.ascontiguousarray(np.asarray(L0_ffn_c_proj, np.float32).T)
    dtb = np.asarray(L0_attn_dt_bias, np.float32)[perm]
    dtb = np.ascontiguousarray(dtb.reshape(DT, 128).T)
    cw = np.asarray(L0_attn_conv_w, np.float32).reshape(D, 4)
    cw = np.ascontiguousarray(cw.reshape(DT, 128, 4).transpose(1, 0, 2).reshape(128, DT * 4))
    cb = np.ascontiguousarray(np.asarray(L0_attn_conv_b, np.float32).reshape(DT, 128).T)
    ns = np.ascontiguousarray(np.asarray(L0_attn_norm_scale, np.float32).reshape(DT, 128).T)
    mix = np.ascontiguousarray(np.asarray(L0_attn_head_mix, np.float32))
    ident = np.eye(128, dtype=BF)
    invfreq = (1.0 / (10000.0 ** (np.arange(0, 128, 2, dtype=np.float32) / 128.0)))

    per_core = []
    for core in range(NCORE):
        b, q = core // 4, core % 4
        start = S * q
        pad = NSTRIP * S - S - start
        xs = np.zeros((NSTRIP * S, D), np.float32)
        xs[pad:] = x[b, 0:start + S]
        xcore = np.ascontiguousarray(
            xs.T.reshape(DT, 128, NSTRIP, S).transpose(1, 2, 0, 3).reshape(128, NSTRIP * DT * S))

        pos = np.zeros(NSTRIP * S, np.float32)
        pos[pad:] = np.arange(0, start + S, dtype=np.float32)
        ang = pos[None, :] * invfreq[np.arange(1024) // 16][:, None]
        costab = np.cos(ang).astype(np.float32)
        sintab = np.sin(ang).astype(np.float32)

        cmask = np.zeros(NSL, np.float32)
        first_real = pad // CH
        cmask[first_real:] = 1.0
        onehot = np.zeros(NSL, np.float32)
        onehot[first_real] = 1.0
        cmask_t = np.ascontiguousarray(np.broadcast_to(np.tile(cmask, DT), (128, DT * NSL)))
        onehot_t = np.ascontiguousarray(np.broadcast_to(np.tile(onehot, DT), (128, DT * NSL)))

        per_core.append({
            "x": xcore, "wint": wint, "woT": woT, "w1": w1, "w2t": w2t,
            "costab": costab, "sintab": sintab, "dtb": dtb, "cw": cw,
            "cb": cb, "ns": ns, "mix": mix, "ident": ident,
            "cmask": cmask_t, "onehot": onehot_t,
            "cpool": np.ascontiguousarray(np.broadcast_to(
                np.asarray(CPOOL, np.float32), (128, len(CPOOL)))),
        })
    return per_core



def build_program(upto="full"):
    nc = bass.Bass("TRN2", target_bir_lowering=False)
    k = KB(nc)

    io = {}
    io["x"] = nc.dram_tensor("x", [128, NSTRIP * DT * S], F32, kind="ExternalInput").ap()
    io["wint"] = nc.dram_tensor("wint", [128, 48 * DT * 128], F32, kind="ExternalInput").ap()
    io["woT"] = nc.dram_tensor("woT", [128, DT * DT * 128], F32, kind="ExternalInput").ap()
    io["w1"] = nc.dram_tensor("w1", [128, 32 * DT * 256], F32, kind="ExternalInput").ap()
    io["w2t"] = nc.dram_tensor("w2t", [H, D], F32, kind="ExternalInput").ap()
    io["costab"] = nc.dram_tensor("costab", [1024, NSTRIP * S], F32, kind="ExternalInput").ap()
    io["sintab"] = nc.dram_tensor("sintab", [1024, NSTRIP * S], F32, kind="ExternalInput").ap()
    io["dtb"] = nc.dram_tensor("dtb", [128, DT], F32, kind="ExternalInput").ap()
    io["cw"] = nc.dram_tensor("cw", [128, DT * 4], F32, kind="ExternalInput").ap()
    io["cb"] = nc.dram_tensor("cb", [128, DT], F32, kind="ExternalInput").ap()
    io["ns"] = nc.dram_tensor("ns", [128, DT], F32, kind="ExternalInput").ap()
    io["mix"] = nc.dram_tensor("mix", [16, 16], F32, kind="ExternalInput").ap()
    io["ident"] = nc.dram_tensor("ident", [128, 128], BF16, kind="ExternalInput").ap()
    io["cmask"] = nc.dram_tensor("cmask", [128, DT * NSL], F32, kind="ExternalInput").ap()
    io["onehot"] = nc.dram_tensor("onehot", [128, DT * NSL], F32, kind="ExternalInput").ap()
    io["cpool"] = nc.dram_tensor("cpool", [128, len(CPOOL)], F32, kind="ExternalInput").ap()
    io["out"] = nc.dram_tensor("out", [128, DT * S], F32, kind="ExternalOutput").ap()
    for name, (shape, dt_) in DBG.items():
        io["dbg_" + name] = nc.dram_tensor("dbg_" + name, list(shape), dt_,
                                           kind="ExternalOutput").ap()

    scr = dict(
        q1=nc.dram_tensor("q1_hbm", [32, 128, DT * 256], BF16),
        q2t=nc.dram_tensor("q2t_hbm", [64, 128, DT * 128], BF16),
        winbf=nc.dram_tensor("winbf_hbm", [48, 128, DT * 128], BF16),
        reluh=nc.dram_tensor("reluh_hbm", [NW, TPS, 128, WJ], F32),
        co=nc.dram_tensor("co_hbm", [DT, 128, S], F32),
        mc=nc.dram_tensor("mc_hbm", [DT, 128, S], F32),
        g=nc.dram_tensor("g_hbm", [DT, 128, S], F32),
    )
    scrbuf = {name: Buf("hbm:" + name) for name in scr}
    st = {"io": io, "scr": scr, "scrbuf": scrbuf, "dbg_done": set(), "upto": upto}

    with ExitStack() as stack:
        e = stack.enter_context
        XRES = e(nc.sbuf_tensor("XRES", [128, DT * S], F32))
        FPA_t = e(nc.sbuf_tensor("FPA", [128, 14336], F32))
        BFA_t = e(nc.sbuf_tensor("BFA", [128, 24576], BF16))
        SM_t = e(nc.sbuf_tensor("SM", [128, 2560], F32))
        SMB_t = e(nc.sbuf_tensor("SMB", [128, 3072], BF16))
        RW_t = e(nc.sbuf_tensor("RW", [1, 4096], F32))
        I32_t = e(nc.sbuf_tensor("I32T", [128, WJ // 2], I32))
        pb = [e(nc.psum_tensor(f"pb{i}", [128, 512], F32)) for i in range(6)]
        tp = [e(nc.psum_tensor(f"tp{i}", [128, 1024], BF16)) for i in range(2)]
        for s in ("pe", "dve", "act", "pool"):
            k.sems[s] = e(nc.semaphore(s))
        for i in range(KB.NDMACH):
            k.sems[f"dq{i}"] = e(nc.semaphore(f"dq{i}"))
        block = e(nc.Block())

        st["fpa"] = Arena(FPA_t, 14336, "fpa", min_class=2048)
        st["bfa"] = Arena(BFA_t, 24576, "bfa", min_class=2048)
        st["sm"] = Arena(SM_t, 2560, "sm", min_class=4)
        st["smb"] = Arena(SMB_t, 3072, "smb", min_class=64)
        st["rw"] = Arena(RW_t, 4096, "rw", min_class=1024)
        st["pb"] = [(p, Buf(f"pb{i}")) for i, p in enumerate(pb)]
        st["tp"] = [(p, Buf(f"tp{i}")) for i, p in enumerate(tp)]
        xa = Arena(XRES, DT * S, "xres")
        st["xres"] = xa.alloc(DT * S, "xres")
        i32a = Arena(I32_t, WJ // 2, "i32a", min_class=512)
        st["i32c"] = i32a.alloc(WJ // 2, "i32").carve([512, 512])

        build_body(k, st)
        k.replay(block)
    return nc, st


def dump(k, st, name, ap, buf):
    if ("dbg_" + name) not in st["io"] or name in st["dbg_done"]:
        return
    st["dbg_done"].add(name)
    k.dma("sync", st["io"]["dbg_" + name], ap, reads=[buf], writes=[Buf("d" + name)], ch="dbg")



def load_consts(k, st):
    io, sm, smb = st["io"], st["sm"], st["smb"]
    c = {}
    for name, cols in (("dtb", DT), ("cw", DT * 4), ("cb", DT), ("ns", DT),
                       ("cmask", DT * NSL), ("onehot", DT * NSL)):
        r = sm.alloc(cols, name)
        k.dma("sync", r.ap(), io[name], writes=[r.buf], ch="init")
        c[name] = r
    cp = sm.alloc(len(CPOOL), "cpool")
    k.dma("sync", cp.ap(), io["cpool"], writes=[cp.buf], ch="init")
    for i, v in enumerate(CPOOL):
        k.nc.const_aps.register(F32, float(v), cp.ap(c0=i, cols=1))
    for eng in ("pe", "dve", "act", "gpsimd"):
        k.barrier_read(eng, [cp.buf])
    c["cpool"] = cp
    idb = smb.alloc(128, "ident")
    k.dma("sync", idb.ap(), io["ident"], writes=[idb.buf], ch="init")
    c["ident"] = idb
    onesf = sm.alloc(256, "onesf")
    k.op("dve", lambda h: h.memset(onesf.ap(), 1.0), writes=[onesf.buf])
    c["onesf"] = onesf
    onesb = smb.alloc(128, "onesb")
    k.op("dve", lambda h: h.memset(onesb.ap(), 1.0), writes=[onesb.buf])
    c["onesb"] = onesb
    mixs = sm.alloc(16, "mix")
    k.dma("sync", mixs.ap(p0=0, pn=16, cols=16), io["mix"], writes=[mixs.buf], ch="init")
    bdf = sm.alloc(128, "bdmixf")
    k.op("dve", lambda h: h.memset(bdf.ap(), 0.0), writes=[bdf.buf])
    for blk in range(8):
        dst = sm.t[16 * blk:16 * blk + 16, bdf.off + 16 * blk:bdf.off + 16 * blk + 16]
        k.dma("sync", dst, mixs.ap(p0=0, pn=16, cols=16),
              reads=[mixs.buf], writes=[bdf.buf], ch="init")
    bdb = smb.alloc(128, "bdmix")
    k.op("dve", lambda h: h.tensor_copy(out=bdb.ap(), in_=bdf.ap()),
         reads=[bdf.buf], writes=[bdb.buf])
    c["bdmix"] = bdb
    bdf.free()
    cwns = sm.alloc(DT * 4, "cwns")
    for dti in range(DT):
        k.op("dve", lambda h, dti=dti: h.tensor_scalar(
            out=cwns.ap(c0=dti * 4, cols=4), in0=c["cw"].ap(c0=dti * 4, cols=4),
            scalar1=c["ns"].ap(c0=dti, cols=1), scalar2=None, op0=ALU.mult),
            reads=[c["cw"].buf, c["ns"].buf], writes=[cwns.buf])
    c["cwns"] = cwns
    bns = sm.alloc(DT, "bns")
    nbns = sm.alloc(DT, "nbns")
    k.op("act", lambda h: h.activation(out=bns.ap(), in_=c["ns"].ap(),
                                       func=AF.Abs), reads=[c["ns"].buf],
         writes=[bns.buf])
    k.op("dve", lambda h: h.reciprocal(out=bns.ap(), in_=bns.ap()),
         reads=[bns.buf], writes=[bns.buf])
    k.op("dve", lambda h: h.tensor_scalar(out=bns.ap(), in0=bns.ap(),
                                          scalar1=60000.0, scalar2=None,
                                          op0=ALU.mult), reads=[bns.buf],
         writes=[bns.buf])
    k.op("dve", lambda h: h.tensor_scalar(out=nbns.ap(), in0=bns.ap(),
                                          scalar1=-1.0, scalar2=None,
                                          op0=ALU.mult), reads=[bns.buf],
         writes=[nbns.buf])
    c["bns"] = bns
    c["nbns"] = nbns
    return c


def wsum_stream(k, st, c, w_io, name, nch=4):
    sm, fpa = st["sm"], st["fpa"]
    acc = sm.alloc(1, name + "_acc")
    k.op("dve", lambda h: h.memset(acc.ap(), 0.0), writes=[acc.buf])
    flat = w_io.rearrange("a b -> (a b)")
    CC = 2048
    nblk = (D * H) // (128 * CC)
    bufs = [fpa.alloc(CC, f"{name}w{i}") for i in range(nch)]
    red = sm.alloc(1, name + "_red")
    for i in range(nblk):
        bb = bufs[i % nch]
        srcap = flat[i * 128 * CC:(i + 1) * 128 * CC].rearrange("(p c) -> p c", p=128)
        k.dma("sync", bb.ap(), srcap, writes=[bb.buf], ch=f"{name}s{i % nch}")
        k.op("dve", lambda h, bb=bb: h.tensor_reduce(
            out=red.ap(), in_=bb.ap(), axis=AX.X, op=ALU.add,
            apply_absolute_value=True), reads=[bb.buf], writes=[red.buf])
        k.op("dve", lambda h: h.tensor_tensor(
            out=acc.ap(), in0=acc.ap(), in1=red.ap(), op=ALU.add),
            reads=[red.buf, acc.buf], writes=[acc.buf])
    for b in bufs:
        b.free()
    red.free()
    return acc


def wsum_finish(k, st, c, acc, name):
    sm = st["sm"]
    (p0, p0b) = st["pb"][5]
    k.op("pe", lambda h: h.matmul(p0[0:1, 0:1], c["onesf"].ap(cols=1),
                                  acc.ap(), start=True, stop=True),
         reads=[c["onesf"].buf, acc.buf], writes=[p0b])
    tot = sm.alloc(1, name + "_tot")
    k.op("dve", lambda h: h.tensor_copy(out=tot.ap(p0=0, pn=1), in_=p0[0:1, 0:1]),
         reads=[p0b], writes=[tot.buf])
    k.op("dve", lambda h: h.tensor_scalar(
        out=tot.ap(p0=0, pn=1), in0=tot.ap(p0=0, pn=1),
        scalar1=1.0 / (D * H), scalar2=EPS_Q, op0=ALU.mult, op1=ALU.max),
        reads=[tot.buf], writes=[tot.buf])
    k.op("pe", lambda h: h.matmul(p0[:, 1:2], c["onesf"].ap(p0=0, pn=1, cols=128),
                                  tot.ap(p0=0, pn=1), start=True, stop=True),
         reads=[c["onesf"].buf, tot.buf], writes=[p0b])
    sc = sm.alloc(4, name)
    k.op("dve", lambda h: h.tensor_copy(out=sc.ap(cols=1), in_=p0[:, 1:2]),
         reads=[p0b], writes=[sc.buf])
    k.op("dve", lambda h: h.tensor_scalar(
        out=sc.ap(c0=1, cols=1), in0=sc.ap(cols=1), scalar1=0.5, scalar2=None,
        op0=ALU.mult), reads=[sc.buf], writes=[sc.buf])
    k.op("dve", lambda h: h.tensor_scalar(
        out=sc.ap(c0=2, cols=1), in0=sc.ap(c0=1, cols=1), scalar1=-1.0,
        scalar2=None, op0=ALU.mult), reads=[sc.buf], writes=[sc.buf])
    acc.free()
    tot.free()
    return sc


def ln_stats(k, st, c, xreg, tag):
    rw, bfa = st["rw"], st["bfa"]
    banks = [st["pb"][i] for i in (2, 3, 4, 5)]
    sqs = [st["smb"].alloc(512, tag + f"sq{i}") for i in range(2)]
    xb2s = [st["smb"].alloc(512, tag + f"xb2_{i}") for i in range(2)]
    for dti in range(DT):
        for half in range(2):
            sq = sqs[(2 * dti + half) % 2]
            xb2 = xb2s[(2 * dti + half) % 2]
            k.op("act", lambda h, dti=dti, half=half, xb2=xb2: h.activation(
                out=xb2.ap(), in_=xreg.ap(c0=dti * S + half * 512, cols=512),
                func=AF.Copy), reads=[xreg.buf], writes=[xb2.buf])
            k.op("act", lambda h, dti=dti, half=half, sq=sq: h.activation(
                out=sq.ap(), in_=xreg.ap(c0=dti * S + half * 512, cols=512),
                func=AF.Square), reads=[xreg.buf], writes=[sq.buf])

            def emit2(h, dti=dti, half=half, sq=sq, xb2=xb2):
                h.matmul(banks[half][0][0:1, 0:512],
                         c["onesb"].ap(cols=1), xb2.ap(cols=512),
                         start=(dti == 0), stop=(dti == DT - 1))
                return h.matmul(banks[2 + half][0][0:1, 0:512],
                                c["onesb"].ap(cols=1), sq.ap(cols=512),
                                start=(dti == 0), stop=(dti == DT - 1))
            k.op("pe", emit2, reads=[c["onesb"].buf, sq.buf, xb2.buf],
                 writes=[banks[half][1], banks[2 + half][1]])
    for b in sqs + xb2s:
        b.free()
    m = rw.alloc(S, tag + "m")
    var = rw.alloc(S, tag + "v")
    r = rw.alloc(S, tag + "r")
    RO = lambda reg, c0=0, n=S: reg.ap(p0=0, pn=1, c0=c0, cols=n)
    for half in range(2):
        k.op("dve", lambda h, half=half: h.tensor_scalar(
            out=RO(m, half * 512, 512), in0=banks[half][0][0:1, 0:512],
            scalar1=1.0 / D, scalar2=None, op0=ALU.mult),
            reads=[banks[half][1]], writes=[m.buf])
        k.op("dve", lambda h, half=half: h.tensor_scalar(
            out=RO(var, half * 512, 512), in0=banks[2 + half][0][0:1, 0:512],
            scalar1=1.0 / D, scalar2=None, op0=ALU.mult),
            reads=[banks[2 + half][1]], writes=[var.buf])
    mm = rw.alloc(S, tag + "mm")
    k.op("dve", lambda h: h.tensor_tensor(out=RO(mm), in0=RO(m), in1=RO(m),
                                          op=ALU.mult), reads=[m.buf], writes=[mm.buf])
    k.op("dve", lambda h: h.tensor_tensor(out=RO(var), in0=RO(var), in1=RO(mm),
                                          op=ALU.subtract),
         reads=[var.buf, mm.buf], writes=[var.buf])
    mm.free()
    k.op("act", lambda h: h.activation(out=RO(r), in_=RO(var), func=AF.Ln,
                                       bias=EPS_LN), reads=[var.buf], writes=[r.buf])
    k.op("act", lambda h: h.activation(out=RO(r), in_=RO(r), func=AF.Exp,
                                       scale=-0.5), reads=[r.buf], writes=[r.buf])
    return {"m": m, "var": var, "r": r}


def bcast_rows(k, st, c, rows, tag):
    out = []
    bank_ids = [(2, 3), (4, 5)]
    for i, reg in enumerate(rows):
        (bA, bAb) = st["pb"][bank_ids[i][0]]
        (bB, bBb) = st["pb"][bank_ids[i][1]]

        def emit(h, reg=reg, bA=bA, bB=bB):
            h.matmul(bA[:, 0:512], c["onesf"].ap(p0=0, pn=1, cols=128),
                     reg.ap(p0=0, pn=1, cols=512), start=True, stop=True)
            return h.matmul(bB[:, 0:512], c["onesf"].ap(p0=0, pn=1, cols=128),
                            reg.ap(p0=0, pn=1, c0=512, cols=512), start=True, stop=True)
        k.op("pe", emit, reads=[reg.buf, c["onesf"].buf], writes=[bAb, bBb])

        def getter(n, bA=bA, bB=bB):
            return (bA if n == 0 else bB)[:, 0:512]
        out.append((getter, [bAb, bBb]))
    return out


def ternary_quant(k, st, sc, wreg, qap_fn, qbuf, tag):
    fpa = st["fpa"]
    ga = fpa.alloc(wreg.cols, tag + "a")
    gb = fpa.alloc(wreg.cols, tag + "b")
    k.op("dve", lambda h: h.tensor_scalar(
        out=ga.ap(), in0=wreg.ap(), scalar1=sc.ap(c0=1, cols=1), scalar2=None,
        op0=ALU.is_ge), reads=[wreg.buf, sc.buf], writes=[ga.buf])
    k.op("dve", lambda h: h.tensor_scalar(
        out=gb.ap(), in0=wreg.ap(), scalar1=sc.ap(c0=2, cols=1), scalar2=None,
        op0=ALU.is_le), reads=[wreg.buf, sc.buf], writes=[gb.buf])
    k.op("dve", lambda h: h.tensor_tensor(
        out=qap_fn(), in0=ga.ap(), in1=gb.ap(), op=ALU.subtract),
        reads=[ga.buf, gb.buf], writes=[qbuf])
    ga.free()
    gb.free()


def ffn_strip(k, st, c, sc1, sc2, strip, jit):
    io, scr, scrbuf = st["io"], st["scr"], st["scrbuf"]
    fpa, bfa, sm = st["fpa"], st["bfa"], st["sm"]
    xres = st["xres"]
    tg = f"f{strip}"

    stats = ln_stats(k, st, c, xres, tg)
    xbf = bfa.alloc(DT * S, tg + "xbf")
    tmp = fpa.alloc(S, tg + "xtmp")
    bc = bcast_rows(k, st, c, [stats["m"], stats["r"]], tg + "bc")
    for dti in range(DT):
        for n in range(2):
            k.op("dve", lambda h, dti=dti, n=n: h.tensor_tensor(
                out=tmp.ap(c0=n * 512, cols=512),
                in0=xres.ap(c0=dti * S + n * 512, cols=512),
                in1=bc[0][0](n), op=ALU.subtract),
                reads=[xres.buf] + bc[0][1], writes=[tmp.buf])
            k.op("dve", lambda h, dti=dti, n=n: h.tensor_tensor(
                out=xbf.ap(c0=dti * S + n * 512, cols=512),
                in0=tmp.ap(c0=n * 512, cols=512),
                in1=bc[1][0](n), op=ALU.mult),
                reads=[tmp.buf] + bc[1][1], writes=[xbf.buf])
    tmp.free()
    stats["m"].free(); stats["var"].free(); stats["r"].free()

    stg = [fpa.alloc(256, tg + f"st{i}") for i in range(2)]
    rmxa = sm.alloc(TPS * 32, tg + "rmxa")
    q1c = [bfa.alloc(DT * 256, tg + f"q1_{i}") for i in range(2)]
    wst = [fpa.alloc(DT * 256, tg + "ws_0")] if jit else None

    for w in range(NW):
        for jc in range(8):
            cidx = w * 8 + jc
            qb = q1c[cidx % 2]
            if jit:
                wb = wst[0]
                k.dma("sync", wb.ap(),
                      io["w1"][:, cidx * DT * 256:(cidx + 1) * DT * 256],
                      writes=[wb.buf], ch="w1s")
                for hh in range(2):
                    wh = Region(wb.arena, wb.off + hh * 2048, 2048, wb.buf)
                    ternary_quant(
                        k, st, sc1, wh,
                        lambda hh=hh, qb=qb: qb.ap(c0=hh * 2048, cols=2048),
                        qb.buf, tg + "q1")
                k.dma("act", scr["q1"].ap()[cidx], qb.ap(),
                      reads=[qb.buf], writes=[scrbuf["q1"]], ch="q1st")
            else:
                k.dma("sync", qb.ap(), scr["q1"].ap()[cidx],
                      reads=[scrbuf["q1"]], writes=[qb.buf], ch=f"q1l{cidx % 2}")
            for t in range(TPS):
                (ph, phb) = st["pb"][t % 2]
                sg = stg[t % 2]

                def emitm(h, t=t, qb=qb, ph=ph):
                    last = None
                    for dti in range(DT):
                        last = h.matmul(ph[:, 0:256],
                                        xbf.ap(c0=dti * S + t * 128, cols=128),
                                        qb.ap(c0=dti * 256, cols=256),
                                        start=(dti == 0), stop=(dti == DT - 1))
                    return last
                k.op("pe", emitm, reads=[xbf.buf, qb.buf], writes=[phb])
                k.op("act", lambda h, ph=ph, sg=sg: h.activation(
                    out=sg.ap(), in_=ph[:, 0:256],
                    func=AF.Relu, scale=sc1.ap(cols=1)),
                    reads=[phb, sc1.buf], writes=[sg.buf])
                k.dma("act", scr["reluh"].ap()[w, t, :, jc * 256:(jc + 1) * 256],
                      sg.ap(), reads=[sg.buf], writes=[scrbuf["reluh"]], ch=f"rhs{t % 2}")
                k.op("dve", lambda h, sg=sg, t=t, cidx=cidx: h.reduce_max(
                    out=rmxa.ap(c0=t * 32 + cidx, cols=1), in_=sg.ap(), axis=AX.X),
                    reads=[sg.buf], writes=[rmxa.buf])
    for b in q1c:
        b.free()
    if wst:
        for b in wst:
            b.free()
    xbf.free()
    for b in stg:
        b.free()
    rmx = sm.alloc(TPS, tg + "rmx")
    for t in range(TPS):
        k.op("dve", lambda h, t=t: h.reduce_max(
            out=rmx.ap(c0=t, cols=1), in_=rmxa.ap(c0=t * 32, cols=32), axis=AX.X),
            reads=[rmxa.buf], writes=[rmx.buf])
    rmxa.free()

    xm = sm.alloc(TPS, tg + "xm")
    squ = sm.alloc(TPS, tg + "squ")
    sh = sm.alloc(TPS, tg + "sh")
    k.op("dve", lambda h: h.tensor_tensor(out=xm.ap(), in0=rmx.ap(), in1=rmx.ap(),
                                          op=ALU.mult), reads=[rmx.buf], writes=[xm.buf])
    k.op("dve", lambda h: h.tensor_scalar(out=xm.ap(), in0=xm.ap(), scalar1=EPS_Q,
                                          scalar2=None, op0=ALU.max),
         reads=[xm.buf], writes=[xm.buf])
    k.op("dve", lambda h: h.reciprocal(out=squ.ap(), in_=xm.ap()),
         reads=[xm.buf], writes=[squ.buf])
    k.op("dve", lambda h: h.tensor_scalar(out=squ.ap(), in0=squ.ap(), scalar1=255.0,
                                          scalar2=None, op0=ALU.mult),
         reads=[squ.buf], writes=[squ.buf])
    k.op("act", lambda h: h.activation(out=squ.ap(), in_=squ.ap(), func=AF.Ln),
         reads=[squ.buf], writes=[squ.buf])
    k.op("act", lambda h: h.activation(out=squ.ap(), in_=squ.ap(), func=AF.Exp,
                                       scale=0.5), reads=[squ.buf], writes=[squ.buf])
    k.op("dve", lambda h: h.tensor_scalar(out=sh.ap(), in0=xm.ap(),
                                          scalar1=1.0 / 255.0, scalar2=None,
                                          op0=ALU.mult), reads=[xm.buf], writes=[sh.buf])
    rmx.free()
    xm.free()
    dsh = st["smb"].alloc(TPS * 128, tg + "dsh")
    for t in range(TPS):
        k.op("dve", lambda h, t=t: h.tensor_scalar(
            out=dsh.ap(c0=t * 128, cols=128), in0=c["ident"].ap(),
            scalar1=sh.ap(c0=t, cols=1), scalar2=None, op0=ALU.mult),
            reads=[c["ident"].buf, sh.buf], writes=[dsh.buf])
    sh.free()

    if jit and "sc2_fix" in st:
        st["sc2_fix"]()
        del st["sc2_fix"]
    hqT = bfa.alloc(DT * S, tg + "hqT")
    q2c = [bfa.alloc(DT * 128, tg + f"q2_{i}") for i in range(2)]
    w2st = fpa.alloc(DT * 128, tg + "w2s") if jit else None

    for w in range(NW):
        if jit:
            for jl in range(DT):
                jt = w * DT + jl
                src = io["w2t"][jt * 128:(jt + 1) * 128, :]
                k.dma("sync", w2st.ap(), src, writes=[w2st.buf], ch="w2s")
                qq = bfa.alloc(DT * 128, tg + "qq2")
                ternary_quant(k, st, sc2, w2st, lambda qq=qq: qq.ap(), qq.buf,
                              tg + "q2")
                k.dma("act", scr["q2t"].ap()[jt], qq.ap(),
                      reads=[qq.buf], writes=[scrbuf["q2t"]], ch="q2st")
                qq.free()
        i32c = st["i32c"]
        rbs = [fpa.alloc(WJ, tg + f"rb{i}") for i in range(2)]
        ubs = [fpa.alloc(WJ, tg + f"u{i}") for i in range(2)]
        fbs = [fpa.alloc(WJ, tg + f"f{i}") for i in range(2)]
        rbfs = [bfa.alloc(WJ, tg + f"rq{i}") for i in range(2)]
        i32c = st["i32c"]
        if "hqhalf" not in st:
            st["hqhalf"] = [Buf("hqA"), Buf("hqB")]
        hqA, hqB = st["hqhalf"]
        for half in range(2):
            hqbuf = (hqA, hqB)[half]
            for t in range(half * 4, half * 4 + 4):
                rbb = rbs[t % 2]
                ub = ubs[t % 2]
                fb = fbs[t % 2]
                rbf = rbfs[t % 2]
                k.dma("sync", rbb.ap(), scr["reluh"].ap()[w, t],
                      reads=[scrbuf["reluh"]], writes=[rbb.buf], ch=f"rhl{t % 2}")
                k.op("act", lambda h, rbb=rbb, t=t, ub=ub: h.activation(
                    out=ub.ap(), in_=rbb.ap(), func=AF.Square,
                    scale=squ.ap(c0=t, cols=1)),
                    reads=[rbb.buf, squ.buf], writes=[ub.buf])
                i3 = i32c[t % 2]
                for hf in range(4):
                    k.op("dve", lambda h, ub=ub, hf=hf, i3=i3: h.tensor_copy(
                        out=i3.arena.t[:, i3.off:i3.off + 512],
                        in_=ub.ap(c0=hf * 512, cols=512)),
                        reads=[ub.buf], writes=[i3.buf])
                    k.op("dve", lambda h, fb=fb, hf=hf, i3=i3: h.tensor_copy(
                        out=fb.ap(c0=hf * 512, cols=512),
                        in_=i3.arena.t[:, i3.off:i3.off + 512]),
                        reads=[i3.buf], writes=[fb.buf])
                k.op("gpsimd", lambda h, ub=ub, fb=fb: h.tensor_tensor(
                    out=ub.ap(), in0=ub.ap(), in1=fb.ap(), op=ALU.subtract),
                    reads=[ub.buf, fb.buf], writes=[ub.buf])
                k.op("dve", lambda h, ub=ub: h.tensor_scalar(
                    out=ub.ap(), in0=ub.ap(), scalar1=0.5, scalar2=None,
                    op0=ALU.is_ge), reads=[ub.buf], writes=[ub.buf])
                k.op("dve", lambda h, ub=ub, fb=fb, rbf=rbf: h.tensor_tensor(
                    out=rbf.ap(), in0=fb.ap(), in1=ub.ap(), op=ALU.add),
                    reads=[fb.buf, ub.buf], writes=[rbf.buf])
                for g in range(4):
                    (pT, pTb) = st["pb"][4 + (g % 2)]

                    def emitt(h, g=g, t=t, pT=pT, rbf=rbf):
                        last = None
                        for jj in range(4):
                            jl = g * 4 + jj
                            last = h.matmul(pT[:, jj * 128:(jj + 1) * 128],
                                            rbf.ap(c0=jl * 128, cols=128),
                                            dsh.ap(c0=t * 128, cols=128),
                                            start=True, stop=True)
                        return last
                    k.op("pe", emitt, reads=[rbf.buf, dsh.buf], writes=[pTb])
                    k.op("act", lambda h, g=g, t=t, pT=pT: h.activation(
                        out=hqT.ap().rearrange("p (j c) -> p j c", j=DT)
                        [:, g * 4:(g + 1) * 4, t * 128:(t + 1) * 128],
                        in_=pT[:, 0:512].rearrange("p (j c) -> p j c", j=4),
                        func=AF.Copy),
                        reads=[pTb], writes=[hqbuf])
            n = half
            for dti in range(DT):
                qb = q2c[dti % 2]
                for g4 in range(4):
                    k.dma("sync",
                          qb.ap(c0=g4 * 4 * 128, cols=4 * 128)
                          .rearrange("p (j c) -> p j c", j=4),
                          scr["q2t"].ap()[w * DT + g4 * 4:w * DT + (g4 + 1) * 4, :,
                                          dti * 128:(dti + 1) * 128]
                          .rearrange("j p c -> p j c"),
                          reads=[scrbuf["q2t"]], writes=[qb.buf], ch=f"q2l{g4}")
                (po, pob) = st["pb"][2 * (dti % 2) + n % 2]

                def emit2(h, qb=qb, n=n, po=po):
                    last = None
                    for jl in range(DT):
                        last = h.matmul(po[:, 0:512],
                                        qb.ap(c0=jl * 128, cols=128),
                                        hqT.ap(c0=jl * 1024 + n * 512, cols=512),
                                        start=(jl == 0), stop=(jl == DT - 1))
                    return last
                k.op("pe", emit2, reads=[qb.buf, hqbuf], writes=[pob])
                k.op("dve", lambda h, dti=dti, n=n, po=po: h.scalar_tensor_tensor(
                    out=xres.ap(c0=dti * S + n * 512, cols=512), in0=po[:, 0:512],
                    scalar=sc2.ap(c0=3, cols=1),
                    in1=xres.ap(c0=dti * S + n * 512, cols=512),
                    op0=ALU.mult, op1=ALU.add),
                    reads=[pob, sc2.buf, xres.buf], writes=[xres.buf])
        for b in rbs + ubs + fbs + rbfs:
            b.free()
    for b in q2c:
        b.free()
    if w2st:
        w2st.free()
    for hb in st["hqhalf"]:
        if hb.writer:
            hqT.buf.readers.append(hb.writer)
        hqT.buf.readers.extend(hb.readers)
    hqT.free(); dsh.free(); squ.free()


def attn_strip(k, st, c, strip, own):
    io, scr, scrbuf = st["io"], st["scr"], st["scrbuf"]
    fpa, bfa, sm, rw = st["fpa"], st["bfa"], st["sm"], st["rw"]
    xres = st["xres"]
    tg = f"a{strip}"
    jit = (strip == 0)

    stats = ln_stats(k, st, c, xres, tg)
    RO = lambda reg, c0=0, n=S: reg.ap(p0=0, pn=1, c0=c0, cols=n)
    Crow = rw.alloc(S, tg + "C")
    k.op("dve", lambda h: h.tensor_tensor(out=RO(Crow), in0=RO(stats["r"]),
                                          in1=RO(stats["r"]), op=ALU.mult),
         reads=[stats["r"].buf], writes=[Crow.buf])
    k.op("dve", lambda h: h.tensor_tensor(out=RO(Crow), in0=RO(Crow),
                                          in1=RO(stats["var"]), op=ALU.mult),
         reads=[Crow.buf, stats["var"].buf], writes=[Crow.buf])
    stats["var"].free()
    k.op("act", lambda h: h.activation(out=RO(Crow), in_=RO(Crow), func=AF.Ln,
                                       bias=EPS_RMS), reads=[Crow.buf], writes=[Crow.buf])
    k.op("act", lambda h: h.activation(out=RO(Crow), in_=RO(Crow), func=AF.Exp,
                                       scale=-0.5), reads=[Crow.buf], writes=[Crow.buf])
    k.op("dve", lambda h: h.tensor_tensor(out=RO(Crow), in0=RO(Crow),
                                          in1=RO(stats["r"]), op=ALU.mult),
         reads=[Crow.buf, stats["r"].buf], writes=[Crow.buf])
    D2row = rw.alloc(S, tg + "D")
    k.op("dve", lambda h: h.tensor_tensor(out=RO(D2row), in0=RO(stats["m"]),
                                          in1=RO(Crow), op=ALU.mult),
         reads=[stats["m"].buf, Crow.buf], writes=[D2row.buf])
    k.op("dve", lambda h: h.tensor_scalar(out=RO(D2row), in0=RO(D2row),
                                          scalar1=-1.0, scalar2=None, op0=ALU.mult),
         reads=[D2row.buf], writes=[D2row.buf])
    bc = bcast_rows(k, st, c, [Crow, D2row], tg + "bc")
    CbS = fpa.alloc(S, tg + "CbS")
    DbS = fpa.alloc(S, tg + "DbS")
    for n in range(2):
        k.op("act", lambda h, n=n: h.activation(
            out=CbS.ap(c0=n * 512, cols=512), in_=bc[0][0](n), func=AF.Copy),
            reads=bc[0][1], writes=[CbS.buf])
        k.op("act", lambda h, n=n: h.activation(
            out=DbS.ap(c0=n * 512, cols=512), in_=bc[1][0](n), func=AF.Copy),
            reads=bc[1][1], writes=[DbS.buf])
    Crow.free()
    D2row.free()
    if own:
        Arow = rw.alloc(S, "A3")
        Brow = rw.alloc(S, "B3")
        k.op("dve", lambda h: h.tensor_scalar(out=RO(Arow), in0=RO(stats["r"]),
                                              scalar1=1.0, scalar2=None, op0=ALU.add),
             reads=[stats["r"].buf], writes=[Arow.buf])
        k.op("dve", lambda h: h.tensor_tensor(out=RO(Brow), in0=RO(stats["m"]),
                                              in1=RO(stats["r"]), op=ALU.mult),
             reads=[stats["m"].buf, stats["r"].buf], writes=[Brow.buf])
        k.op("dve", lambda h: h.tensor_scalar(out=RO(Brow), in0=RO(Brow),
                                              scalar1=-1.0, scalar2=None, op0=ALU.mult),
             reads=[Brow.buf], writes=[Brow.buf])
        st["Arow"] = Arow
        st["Brow"] = Brow
    stats["m"].free(); stats["r"].free()
    xc = bfa.alloc(DT * S, tg + "xc")
    xnbs = [fpa.alloc(S + 64, tg + f"xn{i}") for i in range(2)]
    caccs = [fpa.alloc(S, tg + f"cacc{i}") for i in range(2)]
    if "halo" not in st:
        st["halo"] = sm.alloc(DT * 3, "halo")
        k.op("dve", lambda h: h.memset(st["halo"].ap(), 0.0), writes=[st["halo"].buf])
    halo = st["halo"]
    for dti in range(DT):
        xnb = xnbs[dti % 2]
        cacc = caccs[dti % 2]
        k.op("dve", lambda h, dti=dti, xnb=xnb: h.tensor_copy(
            out=xnb.ap(c0=0, cols=3), in_=halo.ap(c0=dti * 3, cols=3)),
            reads=[halo.buf], writes=[xnb.buf])
        k.op("gpsimd", lambda h, dti=dti, xnb=xnb: h.tensor_tensor(
            out=xnb.ap(c0=3, cols=S), in0=xres.ap(c0=dti * S, cols=S),
            in1=CbS.ap(), op=ALU.mult),
            reads=[xres.buf, CbS.buf], writes=[xnb.buf])
        k.op("gpsimd", lambda h, dti=dti, xnb=xnb: h.tensor_tensor(
            out=xnb.ap(c0=3, cols=S), in0=xnb.ap(c0=3, cols=S),
            in1=DbS.ap(), op=ALU.add),
            reads=[xnb.buf, DbS.buf], writes=[xnb.buf])
        k.op("dve", lambda h, dti=dti, xnb=xnb: h.tensor_scalar(
            out=xnb.ap(c0=3, cols=S), in0=xnb.ap(c0=3, cols=S),
            scalar1=c["bns"].ap(c0=dti, cols=1),
            scalar2=c["nbns"].ap(c0=dti, cols=1),
            op0=ALU.min, op1=ALU.max),
            reads=[xnb.buf, c["bns"].buf, c["nbns"].buf], writes=[xnb.buf])
        k.op("dve", lambda h, dti=dti, xnb=xnb: h.tensor_copy(
            out=halo.ap(c0=dti * 3, cols=3), in_=xnb.ap(c0=S, cols=3)),
            reads=[xnb.buf], writes=[halo.buf])
        k.op("dve", lambda h, dti=dti, xnb=xnb, cacc=cacc: h.tensor_scalar(
            out=cacc.ap(), in0=xnb.ap(c0=0, cols=S),
            scalar1=c["cwns"].ap(c0=dti * 4, cols=1), scalar2=None, op0=ALU.mult),
            reads=[xnb.buf, c["cwns"].buf], writes=[cacc.buf])
        for kk in range(1, 4):
            k.op("dve", lambda h, dti=dti, kk=kk, xnb=xnb, cacc=cacc:
                 h.scalar_tensor_tensor(
                     out=cacc.ap(), in0=xnb.ap(c0=kk, cols=S),
                     scalar=c["cwns"].ap(c0=dti * 4 + kk, cols=1),
                     in1=cacc.ap(), op0=ALU.mult, op1=ALU.add),
                 reads=[xnb.buf, c["cwns"].buf, cacc.buf], writes=[cacc.buf])
        k.op("act", lambda h, dti=dti, cacc=cacc: h.activation(
            out=xc.ap(c0=dti * S, cols=S), in_=cacc.ap(), func=AF.Silu,
            bias=c["cb"].ap(c0=dti, cols=1)),
            reads=[cacc.buf, c["cb"].buf], writes=[xc.buf])
    for b in xnbs + caccs:
        b.free()
    CbS.free()
    DbS.free()

    if "bdb" not in st:
        st["bdb"] = sm.alloc(DT * NSL, "bdb")
        st["bob"] = sm.alloc(DT * NSL, "bob")
    bdb, bob = st["bdb"], st["bob"]

    winb = [bfa.alloc(DT * 128, tg + f"win{i}") for i in range(3)]
    wfs = fpa.alloc(DT * 128, tg + "wfs") if (jit or own) else None
    ropeb = fpa.alloc(2048, tg + "rope")
    cosb, sinb = ropeb.carve([1024, 1024])
    dtb2 = fpa.alloc(2048, tg + "dt")
    dtt = dtb2.carve([1024, 1024])
    vinb = fpa.alloc(2048, tg + "vin")
    vint = vinb.carve([1024, 1024])
    mcco = fpa.alloc(2048, tg + "mcco")
    mcb, cob = mcco.carve([1024, 1024])
    sscr = fpa.alloc(2048, tg + "ssc")
    sA, sB, sC, sD = sscr.carve([512, 512, 512, 512])
    nm0s = sm.alloc(4, tg + "nm0")
    wn = [0]

    def load_win(jt, direct=False):
        wb = winb[wn[0] % 3]
        wn[0] += 1
        if jit or direct:
            k.dma("sync", wfs.ap(), io["wint"][:, jt * DT * 128:(jt + 1) * DT * 128],
                  writes=[wfs.buf], ch="wis")
            k.op("act", lambda h, wb=wb: h.activation(out=wb.ap(), in_=wfs.ap(),
                                                      func=AF.Copy),
                 reads=[wfs.buf], writes=[wb.buf])
            if not direct:
                k.dma("act", scr["winbf"].ap()[jt], wb.ap(),
                      reads=[wb.buf], writes=[scrbuf["winbf"]], ch="wist")
        else:
            k.dma("sync", wb.ap(), scr["winbf"].ap()[jt],
                  reads=[scrbuf["winbf"]], writes=[wb.buf], ch=f"winl{wn[0] % 3}")
        return wb

    def mm_tile(jt, wb, pb_pair):
        for n in range(2):
            (po, pob) = pb_pair[n]

            def emit(h, wb=wb, n=n, po=po):
                last = None
                for dti in range(DT):
                    last = h.matmul(po[:, 0:512],
                                    wb.ap(c0=dti * 128, cols=128),
                                    xc.ap(c0=dti * S + n * 512, cols=512),
                                    start=(dti == 0), stop=(dti == DT - 1))
                return last
            k.op("pe", emit, reads=[wb.buf, xc.buf], writes=[pob])

    for pair in range(8):
        lo, hi = pair, pair + 8
        for idx, tile in enumerate((lo, hi)):
            wb = load_win(tile)
            mm_tile(tile, wb, [st["pb"][0], st["pb"][1]])
            for n in range(2):
                (po, pob) = st["pb"][n]
                k.op("act", lambda h, po=po, tile=tile: h.activation(
                    out=sA.ap(), in_=po[:, 0:512], func=AF.Exp,
                    bias=c["dtb"].ap(c0=tile, cols=1)),
                    reads=[pob, c["dtb"].buf], writes=[sA.buf])
                k.op("act", lambda h: h.activation(
                    out=sB.ap(), in_=sA.ap(), func=AF.Ln, bias=1.0),
                    reads=[sA.buf], writes=[sB.buf])
                k.op("dve", lambda h, idx=idx, n=n: h.tensor_scalar(
                    out=dtt[idx].ap(c0=n * 512, cols=512), in0=sB.ap(),
                    scalar1=0.001, scalar2=2.0, op0=ALU.max, op1=ALU.min),
                    reads=[sB.buf], writes=[dtt[idx].buf])
        k.dma("sync", cosb.ap(),
              io["costab"][pair * 128:(pair + 1) * 128, strip * S:(strip + 1) * S],
              writes=[cosb.buf], ch="rope0")
        k.dma("sync", sinb.ap(),
              io["sintab"][pair * 128:(pair + 1) * 128, strip * S:(strip + 1) * S],
              writes=[sinb.buf], ch="rope1")
        wb = load_win(16 + lo)
        mm_tile(16 + lo, wb, [st["pb"][0], st["pb"][1]])
        wb = load_win(16 + hi)
        mm_tile(16 + hi, wb, [st["pb"][2], st["pb"][3]])
        for n in range(2):
            (p1, p1b) = st["pb"][n]
            (p2, p2b) = st["pb"][2 + n]
            k.op("dve", lambda h, p1=p1, n=n: h.tensor_tensor(
                out=sA.ap(), in0=p1[:, 0:512], in1=cosb.ap(c0=n * 512, cols=512),
                op=ALU.mult), reads=[p1b, cosb.buf], writes=[sA.buf])
            k.op("dve", lambda h, p2=p2, n=n: h.tensor_tensor(
                out=sB.ap(), in0=p2[:, 0:512], in1=sinb.ap(c0=n * 512, cols=512),
                op=ALU.mult), reads=[p2b, sinb.buf], writes=[sB.buf])
            k.op("dve", lambda h, p1=p1, n=n: h.tensor_tensor(
                out=sC.ap(), in0=p1[:, 0:512], in1=sinb.ap(c0=n * 512, cols=512),
                op=ALU.mult), reads=[p1b, sinb.buf], writes=[sC.buf])
            k.op("dve", lambda h, p2=p2, n=n: h.tensor_tensor(
                out=sD.ap(), in0=p2[:, 0:512], in1=cosb.ap(c0=n * 512, cols=512),
                op=ALU.mult), reads=[p2b, cosb.buf], writes=[sD.buf])
            k.op("gpsimd", lambda h: h.tensor_tensor(
                out=sA.ap(), in0=sA.ap(), in1=sB.ap(), op=ALU.subtract),
                reads=[sA.buf, sB.buf], writes=[sA.buf])
            k.op("gpsimd", lambda h: h.tensor_tensor(
                out=sC.ap(), in0=sC.ap(), in1=sD.ap(), op=ALU.add),
                reads=[sC.buf, sD.buf], writes=[sC.buf])
            k.op("dve", lambda h, n=n: h.tensor_tensor(
                out=vint[0].ap(c0=n * 512, cols=512), in0=sA.ap(),
                in1=dtt[0].ap(c0=n * 512, cols=512), op=ALU.mult),
                reads=[sA.buf, dtt[0].buf], writes=[vint[0].buf])
            k.op("gpsimd", lambda h, n=n: h.tensor_tensor(
                out=vint[1].ap(c0=n * 512, cols=512), in0=sC.ap(),
                in1=dtt[1].ap(c0=n * 512, cols=512), op=ALU.mult),
                reads=[sC.buf, dtt[1].buf], writes=[vint[1].buf])
        if own:
            for idx, tile in enumerate((lo, hi)):
                wb = load_win(32 + tile, direct=True)
                mm_tile(32 + tile, wb, [st["pb"][4], st["pb"][5]])
                for n in range(2):
                    (po, pob) = st["pb"][4 + n]
                    k.op("act", lambda h, po=po: h.activation(
                        out=sA.ap(), in_=po[:, 0:512],
                        func=AF.Tanh, scale=0.5), reads=[pob], writes=[sA.buf])
                    dst = (sC, sD)[n]
                    k.op("dve", lambda h, dst=dst: h.tensor_scalar(
                        out=dst.ap(), in0=sA.ap(),
                        scalar1=0.5, scalar2=0.5, op0=ALU.mult, op1=ALU.add),
                        reads=[sA.buf], writes=[dst.buf])
                k.dma("act", scr["g"].ap()[tile], sscr.ap(c0=1024, cols=1024),
                      reads=[sC.buf, sD.buf], writes=[scrbuf["g"]], ch="gst")
        for idx, tile in enumerate((lo, hi)):
            dtt_i, vin_i = dtt[idx], vint[idx]
            for cc in range(4):
                k.op("dve", lambda h, cc=cc, dtt_i=dtt_i: h.tensor_tensor_scan(
                    out=mcb.ap(c0=cc * 256, cols=256),
                    data0=c["onesf"].ap(cols=256),
                    data1=dtt_i.ap(c0=cc * 256, cols=256),
                    initial=0.0, op0=ALU.mult, op1=ALU.add),
                    reads=[c["onesf"].buf, dtt_i.buf], writes=[mcb.buf])
            k.op("dve", lambda h: h.tensor_scalar(
                out=mcb.ap(), in0=mcb.ap(), scalar1=20.0, scalar2=None,
                op0=ALU.min), reads=[mcb.buf], writes=[mcb.buf])
            for cc in range(4):
                k.op("dve", lambda h, cc=cc: h.tensor_scalar(
                    out=nm0s.ap(c0=cc, cols=1), in0=mcb.ap(c0=cc * 256, cols=1),
                    scalar1=-1.0, scalar2=None, op0=ALU.mult),
                    reads=[mcb.buf], writes=[nm0s.buf])
                dst = (sA, sB)[cc // 2]
                k.op("act", lambda h, cc=cc, dst=dst: h.activation(
                    out=dst.ap(c0=(cc % 2) * 256, cols=256),
                    in_=mcb.ap(c0=cc * 256, cols=256),
                    func=AF.Exp, bias=nm0s.ap(c0=cc, cols=1)),
                    reads=[mcb.buf, nm0s.buf], writes=[dst.buf])
            k.op("gpsimd", lambda h, vin_i=vin_i: h.tensor_tensor(
                out=sC.ap(), in0=sA.ap(), in1=vin_i.ap(cols=512), op=ALU.mult),
                reads=[sA.buf, vin_i.buf], writes=[sC.buf])
            k.op("gpsimd", lambda h, vin_i=vin_i: h.tensor_tensor(
                out=sD.ap(), in0=sB.ap(), in1=vin_i.ap(c0=512, cols=512),
                op=ALU.mult), reads=[sB.buf, vin_i.buf], writes=[sD.buf])
            for cc in range(4):
                zsrc = (sC, sD)[cc // 2]
                dst = (sA, sB)[cc // 2]
                k.op("dve", lambda h, cc=cc, zsrc=zsrc, dst=dst: h.tensor_tensor_scan(
                    out=dst.ap(c0=(cc % 2) * 256, cols=256),
                    data0=c["onesf"].ap(cols=256),
                    data1=zsrc.ap(c0=(cc % 2) * 256, cols=256),
                    initial=0.0, op0=ALU.mult, op1=ALU.add),
                    reads=[c["onesf"].buf, zsrc.buf], writes=[dst.buf])
            for cc in range(4):
                dst = (sC, sD)[cc // 2]
                k.op("act", lambda h, cc=cc, dst=dst: h.activation(
                    out=dst.ap(c0=(cc % 2) * 256, cols=256),
                    in_=mcb.ap(c0=cc * 256, cols=256),
                    func=AF.Exp, scale=-1.0, bias=mcb.ap(c0=cc * 256, cols=1)),
                    reads=[mcb.buf], writes=[dst.buf])
            k.op("gpsimd", lambda h: h.tensor_tensor(
                out=cob.ap(cols=512), in0=sC.ap(), in1=sA.ap(), op=ALU.mult),
                reads=[sC.buf, sA.buf], writes=[cob.buf])
            k.op("gpsimd", lambda h: h.tensor_tensor(
                out=cob.ap(c0=512, cols=512), in0=sD.ap(), in1=sB.ap(),
                op=ALU.mult), reads=[sD.buf, sB.buf], writes=[cob.buf])
            for cc in range(4):
                slot = strip * 4 + cc
                k.op("dve", lambda h, tile=tile, slot=slot, cc=cc: h.tensor_scalar(
                    out=bdb.ap(c0=tile * NSL + slot, cols=1),
                    in0=mcb.ap(c0=cc * 256 + 255, cols=1),
                    scalar1=-1.0, scalar2=None, op0=ALU.mult),
                    reads=[mcb.buf], writes=[bdb.buf])
                k.op("dve", lambda h, tile=tile, slot=slot, cc=cc: h.tensor_copy(
                    out=bob.ap(c0=tile * NSL + slot, cols=1),
                    in_=cob.ap(c0=cc * 256 + 255, cols=1)),
                    reads=[cob.buf], writes=[bob.buf])
            if own:
                k.dma("act", scr["mc"].ap()[tile], mcb.ap(),
                      reads=[mcb.buf], writes=[scrbuf["mc"]], ch="mcst")
                k.dma("act", scr["co"].ap()[tile], cob.ap(),
                      reads=[cob.buf], writes=[scrbuf["co"]], ch="cost")
    
    for b in winb:
        b.free()
    if wfs:
        wfs.free()
    ropeb.free(); dtb2.free(); vinb.free(); mcco.free(); sscr.free()
    nm0s.free()
    xc.free()


def carry_and_finish(k, st, c):
    io, scr, scrbuf = st["io"], st["scr"], st["scrbuf"]
    fpa, bfa, sm = st["fpa"], st["bfa"], st["sm"]
    xres = st["xres"]
    bdb, bob = st["bdb"], st["bob"]

    carries = sm.alloc(DT * NSL, "carries")
    scr2 = sm.alloc(6 * NSL, "cscratch")
    k.op("dve", lambda h: h.tensor_tensor(out=bdb.ap(), in0=bdb.ap(),
                                          in1=c["cmask"].ap(), op=ALU.mult),
         reads=[bdb.buf, c["cmask"].buf], writes=[bdb.buf])
    k.op("dve", lambda h: h.tensor_tensor(out=bob.ap(), in0=bob.ap(),
                                          in1=c["cmask"].ap(), op=ALU.mult),
         reads=[bob.buf, c["cmask"].buf], writes=[bob.buf])
    for jt in range(DT):
        o = jt * NSL
        A = lambda c0, n=NSL: scr2.ap(c0=c0 * NSL, cols=n)
        k.op("dve", lambda h, o=o: h.tensor_tensor_scan(
            out=A(0), data0=c["onesf"].ap(cols=NSL), data1=bdb.ap(c0=o, cols=NSL),
            initial=0.0, op0=ALU.mult, op1=ALU.add),
            reads=[c["onesf"].buf, bdb.buf], writes=[scr2.buf])
        k.op("dve", lambda h: h.tensor_scalar(
            out=A(0), in0=A(0), scalar1=-80.0, scalar2=None, op0=ALU.max),
            reads=[scr2.buf], writes=[scr2.buf])
        k.op("dve", lambda h, o=o: h.tensor_tensor(
            out=A(1), in0=A(0), in1=c["onehot"].ap(c0=o, cols=NSL), op=ALU.mult),
            reads=[scr2.buf, c["onehot"].buf], writes=[scr2.buf])
        k.op("dve", lambda h: h.tensor_reduce(
            out=A(2, 1), in_=A(1), axis=AX.X, op=ALU.add),
            reads=[scr2.buf], writes=[scr2.buf])
        k.op("dve", lambda h: h.tensor_scalar(
            out=A(1), in0=A(0), scalar1=A(2, 1), scalar2=-20.0,
            op0=ALU.subtract, op1=ALU.max),
            reads=[scr2.buf], writes=[scr2.buf])
        k.op("dve", lambda h: h.tensor_scalar(
            out=A(1), in0=A(1), scalar1=0.0, scalar2=None, op0=ALU.min),
            reads=[scr2.buf], writes=[scr2.buf])
        k.op("act", lambda h: h.activation(
            out=A(3), in_=A(1), func=AF.Exp, scale=-1.0),
            reads=[scr2.buf], writes=[scr2.buf])
        k.op("dve", lambda h, o=o: h.tensor_tensor(
            out=A(3), in0=A(3), in1=bob.ap(c0=o, cols=NSL), op=ALU.mult),
            reads=[scr2.buf, bob.buf], writes=[scr2.buf])
        k.op("dve", lambda h: h.memset(A(4, 1), 0.0), writes=[scr2.buf])
        k.op("dve", lambda h: h.tensor_copy(
            out=scr2.ap(c0=4 * NSL + 1, cols=NSL - 1), in_=A(3, NSL - 1)),
            reads=[scr2.buf], writes=[scr2.buf])
        k.op("dve", lambda h: h.tensor_tensor_scan(
            out=A(5), data0=c["onesf"].ap(cols=NSL), data1=A(4),
            initial=0.0, op0=ALU.mult, op1=ALU.add),
            reads=[c["onesf"].buf, scr2.buf], writes=[scr2.buf])
        k.op("act", lambda h: h.activation(out=A(3), in_=A(1), func=AF.Exp),
             reads=[scr2.buf], writes=[scr2.buf])
        k.op("dve", lambda h, jt=jt: h.tensor_tensor(
            out=carries.ap(c0=jt * NSL, cols=NSL), in0=A(5), in1=A(3),
            op=ALU.mult), reads=[scr2.buf], writes=[carries.buf])
    scr2.free()

    bc = bcast_rows(k, st, c, [st["Arow"], st["Brow"]], "fbc")
    for dti in range(DT):
        for n in range(2):
            k.op("dve", lambda h, dti=dti, n=n: h.tensor_tensor(
                out=xres.ap(c0=dti * S + n * 512, cols=512),
                in0=xres.ap(c0=dti * S + n * 512, cols=512),
                in1=bc[0][0](n), op=ALU.mult),
                reads=[xres.buf] + bc[0][1], writes=[xres.buf])
            k.op("dve", lambda h, dti=dti, n=n: h.tensor_tensor(
                out=xres.ap(c0=dti * S + n * 512, cols=512),
                in0=xres.ap(c0=dti * S + n * 512, cols=512),
                in1=bc[1][0](n), op=ALU.add),
                reads=[xres.buf] + bc[1][1], writes=[xres.buf])
    st["Arow"].free()
    st["Brow"].free()

    ub = bfa.alloc(DT * S, "ub")
    finb = bfa.alloc(S, "finb")
    mcl = fpa.alloc(S, "mcl")
    col = fpa.alloc(S, "col")
    gl = fpa.alloc(S, "gl")
    expl = fpa.alloc(S, "expl")
    for jt in range(DT):
        k.dma("sync", mcl.ap(), scr["mc"].ap()[jt], reads=[scrbuf["mc"]],
              writes=[mcl.buf], ch="mcl")
        k.dma("sync", col.ap(), scr["co"].ap()[jt], reads=[scrbuf["co"]],
              writes=[col.buf], ch="col")
        k.dma("sync", gl.ap(), scr["g"].ap()[jt], reads=[scrbuf["g"]],
              writes=[gl.buf], ch="gl")
        k.op("act", lambda h: h.activation(out=expl.ap(), in_=mcl.ap(),
                                           func=AF.Exp, scale=-1.0),
             reads=[mcl.buf], writes=[expl.buf])
        for cc in range(4):
            k.op("dve", lambda h, jt=jt, cc=cc: h.scalar_tensor_tensor(
                out=finb.ap(c0=cc * 256, cols=256),
                in0=expl.ap(c0=cc * 256, cols=256),
                scalar=carries.ap(c0=jt * NSL + 12 + cc, cols=1),
                in1=col.ap(c0=cc * 256, cols=256),
                op0=ALU.mult, op1=ALU.add),
                reads=[expl.buf, carries.buf, col.buf], writes=[finb.buf])
        for n in range(2):
            (po, pob) = st["pb"][n]
            k.op("pe", lambda h, n=n, po=po: h.matmul(
                po[:, 0:512], c["bdmix"].ap(),
                finb.ap(c0=n * 512, cols=512), start=True, stop=True),
                reads=[c["bdmix"].buf, finb.buf], writes=[pob])
            k.op("dve", lambda h, jt=jt, n=n, po=po: h.tensor_tensor(
                out=ub.ap(c0=jt * S + n * 512, cols=512), in0=po[:, 0:512],
                in1=gl.ap(c0=n * 512, cols=512), op=ALU.mult),
                reads=[pob, gl.buf], writes=[ub.buf])
    finb.free(); mcl.free(); col.free(); gl.free(); expl.free()
    carries.free()

    wob = bfa.alloc(DT * 128, "wob")
    wof = fpa.alloc(DT * 128, "wof")
    for dm in range(DT):
        k.dma("sync", wof.ap(), io["woT"][:, dm * DT * 128:(dm + 1) * DT * 128],
              writes=[wof.buf], ch="wol")
        k.op("act", lambda h: h.activation(out=wob.ap(), in_=wof.ap(),
                                           func=AF.Copy),
             reads=[wof.buf], writes=[wob.buf])
        for n in range(2):
            (po, pob) = st["pb"][2 + n]

            def emit(h, n=n, po=po):
                last = None
                for dti in range(DT):
                    last = h.matmul(po[:, 0:512],
                                    wob.ap(c0=dti * 128, cols=128),
                                    ub.ap(c0=dti * S + n * 512, cols=512),
                                    start=(dti == 0), stop=(dti == DT - 1))
                return last
            k.op("pe", emit, reads=[wob.buf, ub.buf], writes=[pob])
            k.op("dve", lambda h, dm=dm, n=n, po=po: h.tensor_tensor(
                out=xres.ap(c0=dm * S + n * 512, cols=512),
                in0=xres.ap(c0=dm * S + n * 512, cols=512),
                in1=po[:, 0:512], op=ALU.add),
                reads=[pob, xres.buf], writes=[xres.buf])
    wob.free(); wof.free(); ub.free()


def build_body(k, st):
    io = st["io"]
    c = load_consts(k, st)
    acc1 = wsum_stream(k, st, c, io["w1"], "sc1")
    sc1 = wsum_finish(k, st, c, acc1, "sc1")
    acc2 = wsum_stream(k, st, c, io["w2t"], "sc2")
    sc2 = st["sm"].alloc(4, "sc2")

    def sc2_fix():
        sc2n = wsum_finish(k, st, c, acc2, "sc2n")
        k.op("dve", lambda h, sc2n=sc2n: h.tensor_scalar(
            out=sc2n.ap(c0=3, cols=1), in0=sc2n.ap(cols=1), scalar1=0.5,
            scalar2=None, op0=ALU.mult), reads=[sc2n.buf], writes=[sc2n.buf])
        sc2.arena, sc2.off, sc2.cols, sc2.buf, sc2.req = \
            sc2n.arena, sc2n.off, sc2n.cols, sc2n.buf, sc2n.req
    st["sc2_fix"] = sc2_fix
    st["consts"] = c
    st["sc1"] = sc1
    st["sc2"] = sc2

    xres = st["xres"]
    nstrip = 1 if st["upto"] == "ffn1strip" else NSTRIP
    for strip in range(nstrip):
        k.dma("sync", xres.ap(),
              io["x"][:, strip * DT * S:(strip + 1) * DT * S],
              writes=[xres.buf], ch="xload")
        ffn_strip(k, st, c, sc1, sc2, strip, jit=(strip == 0))
        if st["upto"] in ("full", "attn"):
            attn_strip(k, st, c, strip, own=(strip == NSTRIP - 1))

    if st["upto"] in ("full", "attn"):
        carry_and_finish(k, st, c)
    if st["upto"] == "full":
        ffn_strip(k, st, c, sc1, sc2, NSTRIP, jit=False)

    k.dma("sync", io["out"], xres.ap(),
          reads=[xres.buf], writes=[Buf("out")], ch="out")
    k.wait_all_dma("gpsimd")



def kernel(**inputs):
    per_core = prep_inputs(**inputs)
    nc, st = build_program()
    res = run_bass_kernel_spmd(nc, per_core, core_ids=list(range(NCORE)))
    out = np.zeros((2, 4096, D), np.float32)
    for core in range(NCORE):
        b, q = core // 4, core % 4
        o = res.results[core]["out"].reshape(128, DT, S).transpose(1, 0, 2)
        out[b, S * q:S * (q + 1)] = o.reshape(D, S).T
    return out


# version 08 (final)
# speedup vs baseline: 1.2075x; 1.2075x over previous
import numpy as np
import ml_dtypes
from contextlib import ExitStack

import concourse.bass as bass
import concourse.mybir as mybir
from concourse.bass_utils import run_bass_kernel_spmd

F32 = mybir.dt.float32
BF16 = mybir.dt.bfloat16
I32 = mybir.dt.int32
AF = mybir.ActivationFunctionType
ALU = mybir.AluOpType
AX = mybir.AxisListType
BF = ml_dtypes.bfloat16

D = 2048
DT = 16
H = 8192
S = 1024
NSTRIP = 4
CH = 256
NSL = 16
TPS = 8
NW = 4
WJ = 2048
EPS_LN = 1e-5
EPS_RMS = 1e-6
EPS_Q = 1e-5
NCORE = 8

DBG = {}

CPOOL = [EPS_LN, EPS_RMS, 1.0, -0.5, 0.5, -1.0, 2.0, 0.001]



class Buf:
    __slots__ = ("name", "writer", "readers")

    def __init__(self, name):
        self.name = name
        self.writer = None
        self.readers = []


class KB:
    ENG = ("sync", "pe", "dve", "act", "gpsimd")
    NDMACH = 40

    def __init__(self, nc):
        self.nc = nc
        self.prog = {e: [] for e in self.ENG}
        self.counts = {s: 0 for s in ("pe", "dve", "act", "pool")}
        self.eng_sem = {"pe": "pe", "dve": "dve", "act": "act", "gpsimd": "pool"}
        self.wm = {}
        self.sems = {}
        self.chmap = {}
        for i in range(self.NDMACH):
            self.counts[f"dq{i}"] = 0

    def _wait(self, eng, token):
        if token is None:
            return
        sem, val = token
        if self.wm.get((eng, sem), 0) >= val:
            return
        if self.eng_sem.get(eng) == sem:
            assert self.counts[sem] >= val, (eng, sem, val, self.counts[sem])
        self.prog[eng].append(("wait", sem, val))
        self.wm[(eng, sem)] = val

    def op(self, eng, emit, reads=(), writes=(), sem=None, inc=1):
        if sem is None:
            sem = self.eng_sem[eng]
        for b in reads:
            self._wait(eng, b.writer)
        for b in writes:
            self._wait(eng, b.writer)
            for t in b.readers:
                self._wait(eng, t)
        self.counts[sem] += inc
        tok = (sem, self.counts[sem])
        self.prog[eng].append(("emit", emit, sem, inc,
                               [(b.name, b.writer) for b in reads],
                               [(b.name, b.writer, list(b.readers)) for b in writes]))
        for b in writes:
            b.writer = tok
            b.readers = []
        for b in reads:
            b.readers.append(tok)
        return tok

    def dma(self, eng, out_ap, in_ap, reads=(), writes=(), ch="misc"):
        if ch not in self.chmap:
            assert len(self.chmap) < self.NDMACH, "out of dma channels"
            self.chmap[ch] = f"dq{len(self.chmap)}"
            self.chowner = getattr(self, "chowner", {})
            self.chowner[ch] = eng
        assert self.chowner[ch] == eng, f"channel {ch} used from {eng} and {self.chowner[ch]}"
        sem = self.chmap[ch]
        self._wait(eng, (sem, self.counts[sem]))

        def emit(h, out_ap=out_ap, in_ap=in_ap):
            return h.dma_start(out=out_ap, in_=in_ap)
        return self.op(eng, emit, reads=reads, writes=writes, sem=sem, inc=16)

    def barrier_read(self, eng, bufs):
        for b in bufs:
            self._wait(eng, b.writer)

    def wait_all_dma(self, eng):
        for ch, sem in self.chmap.items():
            self.prog[eng].append(("wait", sem, self.counts[sem]))

    def replay(self, block):
        sems = self.sems

        def mk(items):
            def body(h):
                for it in items:
                    if it[0] == "wait":
                        h.wait_ge(sems[it[1]], it[2])
                    else:
                        it[1](h).then_inc(sems[it[2]], it[3])
            return body
        block.sync(mk(self.prog["sync"]))
        block.tensor(mk(self.prog["pe"]))
        block.vector(mk(self.prog["dve"]))
        block.scalar(mk(self.prog["act"]))
        block.gpsimd(mk(self.prog["gpsimd"]))



class Arena:
    def __init__(self, tensor, cols, name, min_class=512):
        self.t = tensor
        self.cols = cols
        self.name = name
        self.min_class = min_class
        self.off = 0
        self.freelist = {}
        self.live = {}

    def alloc(self, cols, tag=""):
        req = cols
        cls = self.min_class
        while cls < cols:
            cls *= 2
        cols = cls
        lst = self.freelist.get(cols)
        if lst:
            off, buf = lst.pop()
        else:
            bigger = sorted(sz for sz, l in self.freelist.items() if sz > cols and l)
            if bigger:
                sz = bigger[0]
                off, buf = self.freelist[sz].pop()
                rem = sz - cols
                o2 = off + cols
                while rem:
                    half = cols
                    while half * 2 <= rem:
                        half *= 2
                    nb = Buf(f"{self.name}:{o2}")
                    nb.writer = buf.writer
                    nb.readers = list(buf.readers)
                    self.freelist.setdefault(half, []).append((o2, nb))
                    o2 += half
                    rem -= half
            else:
                al = min(cols, 4096)
                off = (self.off + al - 1) // al * al
                for gap_off in range(self.off, off, self.min_class):
                    gb = Buf(f"{self.name}:{gap_off}")
                    self.freelist.setdefault(self.min_class, []).append((gap_off, gb))
                assert off + cols <= self.cols, \
                    f"arena {self.name} overflow: {off}+{cols}>{self.cols} ({tag})"
                self.off = off + cols
                buf = Buf(f"{self.name}:{off}")
        for lo, ln in self.live.items():
            assert off + cols <= lo or lo + ln <= off, \
                f"arena {self.name}: overlap alloc [{off},{off+cols}) with live [{lo},{lo+ln}) ({tag})"
        self.live[off] = cols
        return Region(self, off, cols, buf, req)

    def release(self, reg):
        assert self.live.pop(reg.off, None) is not None, \
            f"double free {self.name}@{reg.off}"
        off, cols, buf = reg.off, reg.cols, reg.buf
        while cols < 4096:
            buddy = off ^ cols
            lst = self.freelist.get(cols, [])
            hit = next((i for i, (o, _b) in enumerate(lst) if o == buddy), None)
            if hit is None:
                break
            bo, bb = lst.pop(hit)
            nb = Buf(f"{self.name}:{min(off, bo)}")
            nb.readers = [t for t in (buf.writer, bb.writer) if t] \
                + buf.readers + bb.readers
            buf = nb
            off = min(off, bo)
            cols *= 2
        self.freelist.setdefault(cols, []).append((off, buf))


class Region:
    __slots__ = ("arena", "off", "cols", "buf", "req", "kids")

    def __init__(self, arena, off, cols, buf, req=None):
        self.arena = arena
        self.off = off
        self.cols = cols
        self.buf = buf
        self.req = req if req is not None else cols
        self.kids = None

    def carve(self, sizes):
        regs = []
        o = self.off
        for sz in sizes:
            b = Buf(f"{self.buf.name}+{o - self.off}")
            b.writer = self.buf.writer
            b.readers = list(self.buf.readers)
            regs.append(Region(self.arena, o, sz, b))
            o += sz
        assert o <= self.off + self.cols
        self.kids = regs
        return regs

    def ap(self, c0=0, cols=None, p0=0, pn=128):
        cols = self.req - c0 if cols is None else cols
        assert c0 + cols <= self.cols, (c0, cols, self.cols)
        return self.arena.t[p0:p0 + pn, self.off + c0:self.off + c0 + cols]

    def free(self):
        if self.kids:
            for kid in self.kids:
                if kid.buf.writer:
                    self.buf.readers.append(kid.buf.writer)
                self.buf.readers.extend(kid.buf.readers)
            self.kids = None
        self.arena.release(self)



def _mixer_perm():
    cp = np.arange(D)
    dh, hh = cp // 16, cp % 16
    return hh * 128 + dh


def prep_inputs(L0_attn_x, L0_attn_norm_scale, L0_attn_conv_w, L0_attn_conv_b,
                L0_attn_in_proj_w, L0_attn_dt_bias, L0_attn_head_mix,
                L0_attn_out_proj_w, L0_ffn_c_fc, L0_ffn_c_proj):
    perm = _mixer_perm()
    x = np.asarray(L0_attn_x, np.float32)
    wint = np.asarray(L0_attn_in_proj_w, np.float32)
    wint = wint.reshape(3, D, D)[:, perm, :]
    wint = wint.reshape(3 * D, D).T
    wint = np.ascontiguousarray(
        wint.reshape(DT, 128, 48, 128).transpose(1, 2, 0, 3).reshape(128, 48 * DT * 128))
    wo = np.asarray(L0_attn_out_proj_w, np.float32)[:, perm]
    woT = wo.T
    woT = np.ascontiguousarray(
        woT.reshape(DT, 128, DT, 128).transpose(1, 2, 0, 3).reshape(128, DT * DT * 128))
    w1 = np.asarray(L0_ffn_c_fc, np.float32)
    w1 = np.ascontiguousarray(
        w1.reshape(DT, 128, 32, 256).transpose(1, 2, 0, 3).reshape(128, 32 * DT * 256))
    w2t = np.ascontiguousarray(np.asarray(L0_ffn_c_proj, np.float32).T)
    dtb = np.asarray(L0_attn_dt_bias, np.float32)[perm]
    dtb = np.ascontiguousarray(dtb.reshape(DT, 128).T)
    cw = np.asarray(L0_attn_conv_w, np.float32).reshape(D, 4)
    cw = np.ascontiguousarray(cw.reshape(DT, 128, 4).transpose(1, 0, 2).reshape(128, DT * 4))
    cb = np.ascontiguousarray(np.asarray(L0_attn_conv_b, np.float32).reshape(DT, 128).T)
    ns = np.ascontiguousarray(np.asarray(L0_attn_norm_scale, np.float32).reshape(DT, 128).T)
    mix = np.ascontiguousarray(np.asarray(L0_attn_head_mix, np.float32))
    ident = np.eye(128, dtype=BF)
    invfreq = (1.0 / (10000.0 ** (np.arange(0, 128, 2, dtype=np.float32) / 128.0)))

    per_core = []
    for core in range(NCORE):
        b, q = core // 4, core % 4
        start = S * q
        pad = NSTRIP * S - S - start
        xs = np.zeros((NSTRIP * S, D), np.float32)
        xs[pad:] = x[b, 0:start + S]
        xcore = np.ascontiguousarray(
            xs.T.reshape(DT, 128, NSTRIP, S).transpose(1, 2, 0, 3).reshape(128, NSTRIP * DT * S))

        pos = np.zeros(NSTRIP * S, np.float32)
        pos[pad:] = np.arange(0, start + S, dtype=np.float32)
        ang = pos[None, :] * invfreq[np.arange(1024) // 16][:, None]
        costab = np.cos(ang).astype(np.float32)
        sintab = np.sin(ang).astype(np.float32)

        cmask = np.zeros(NSL, np.float32)
        first_real = pad // CH
        cmask[first_real:] = 1.0
        onehot = np.zeros(NSL, np.float32)
        onehot[first_real] = 1.0
        cmask_t = np.ascontiguousarray(np.broadcast_to(np.tile(cmask, DT), (128, DT * NSL)))
        onehot_t = np.ascontiguousarray(np.broadcast_to(np.tile(onehot, DT), (128, DT * NSL)))

        per_core.append({
            "x": xcore, "wint": wint, "woT": woT, "w1": w1, "w2t": w2t,
            "costab": costab, "sintab": sintab, "dtb": dtb, "cw": cw,
            "cb": cb, "ns": ns, "mix": mix, "ident": ident,
            "cmask": cmask_t, "onehot": onehot_t,
            "cpool": np.ascontiguousarray(np.broadcast_to(
                np.asarray(CPOOL, np.float32), (128, len(CPOOL)))),
        })
    return per_core



def build_program(upto="full"):
    nc = bass.Bass("TRN2", target_bir_lowering=False)
    k = KB(nc)

    io = {}
    io["x"] = nc.dram_tensor("x", [128, NSTRIP * DT * S], F32, kind="ExternalInput").ap()
    io["wint"] = nc.dram_tensor("wint", [128, 48 * DT * 128], F32, kind="ExternalInput").ap()
    io["woT"] = nc.dram_tensor("woT", [128, DT * DT * 128], F32, kind="ExternalInput").ap()
    io["w1"] = nc.dram_tensor("w1", [128, 32 * DT * 256], F32, kind="ExternalInput").ap()
    io["w2t"] = nc.dram_tensor("w2t", [H, D], F32, kind="ExternalInput").ap()
    io["costab"] = nc.dram_tensor("costab", [1024, NSTRIP * S], F32, kind="ExternalInput").ap()
    io["sintab"] = nc.dram_tensor("sintab", [1024, NSTRIP * S], F32, kind="ExternalInput").ap()
    io["dtb"] = nc.dram_tensor("dtb", [128, DT], F32, kind="ExternalInput").ap()
    io["cw"] = nc.dram_tensor("cw", [128, DT * 4], F32, kind="ExternalInput").ap()
    io["cb"] = nc.dram_tensor("cb", [128, DT], F32, kind="ExternalInput").ap()
    io["ns"] = nc.dram_tensor("ns", [128, DT], F32, kind="ExternalInput").ap()
    io["mix"] = nc.dram_tensor("mix", [16, 16], F32, kind="ExternalInput").ap()
    io["ident"] = nc.dram_tensor("ident", [128, 128], BF16, kind="ExternalInput").ap()
    io["cmask"] = nc.dram_tensor("cmask", [128, DT * NSL], F32, kind="ExternalInput").ap()
    io["onehot"] = nc.dram_tensor("onehot", [128, DT * NSL], F32, kind="ExternalInput").ap()
    io["cpool"] = nc.dram_tensor("cpool", [128, len(CPOOL)], F32, kind="ExternalInput").ap()
    io["out"] = nc.dram_tensor("out", [128, DT * S], F32, kind="ExternalOutput").ap()
    for name, (shape, dt_) in DBG.items():
        io["dbg_" + name] = nc.dram_tensor("dbg_" + name, list(shape), dt_,
                                           kind="ExternalOutput").ap()

    scr = dict(
        q1=nc.dram_tensor("q1_hbm", [32, 128, DT * 256], BF16),
        q2t=nc.dram_tensor("q2t_hbm", [64, 128, DT * 128], BF16),
        winbf=nc.dram_tensor("winbf_hbm", [48, 128, DT * 128], BF16),
        reluh=nc.dram_tensor("reluh_hbm", [NW, TPS, 128, WJ], F32),
        co=nc.dram_tensor("co_hbm", [DT, 128, S], F32),
        mc=nc.dram_tensor("mc_hbm", [DT, 128, S], F32),
        g=nc.dram_tensor("g_hbm", [DT, 128, S], F32),
    )
    scrbuf = {name: Buf("hbm:" + name) for name in scr}
    st = {"io": io, "scr": scr, "scrbuf": scrbuf, "dbg_done": set(), "upto": upto}

    with ExitStack() as stack:
        e = stack.enter_context
        XRES = e(nc.sbuf_tensor("XRES", [128, DT * S], F32))
        FPA_t = e(nc.sbuf_tensor("FPA", [128, 14336], F32))
        BFA_t = e(nc.sbuf_tensor("BFA", [128, 24576], BF16))
        SM_t = e(nc.sbuf_tensor("SM", [128, 2560], F32))
        SMB_t = e(nc.sbuf_tensor("SMB", [128, 3072], BF16))
        RW_t = e(nc.sbuf_tensor("RW", [1, 4096], F32))
        I32_t = e(nc.sbuf_tensor("I32T", [128, WJ // 2], I32))
        pb = [e(nc.psum_tensor(f"pb{i}", [128, 512], F32)) for i in range(6)]
        tp = [e(nc.psum_tensor(f"tp{i}", [128, 1024], BF16)) for i in range(2)]
        for s in ("pe", "dve", "act", "pool"):
            k.sems[s] = e(nc.semaphore(s))
        for i in range(KB.NDMACH):
            k.sems[f"dq{i}"] = e(nc.semaphore(f"dq{i}"))
        block = e(nc.Block())

        st["fpa"] = Arena(FPA_t, 14336, "fpa", min_class=2048)
        st["bfa"] = Arena(BFA_t, 24576, "bfa", min_class=2048)
        st["sm"] = Arena(SM_t, 2560, "sm", min_class=4)
        st["smb"] = Arena(SMB_t, 3072, "smb", min_class=64)
        st["rw"] = Arena(RW_t, 4096, "rw", min_class=1024)
        st["pb"] = [(p, Buf(f"pb{i}")) for i, p in enumerate(pb)]
        st["tp"] = [(p, Buf(f"tp{i}")) for i, p in enumerate(tp)]
        xa = Arena(XRES, DT * S, "xres")
        st["xres"] = xa.alloc(DT * S, "xres")
        i32a = Arena(I32_t, WJ // 2, "i32a", min_class=512)
        st["i32c"] = i32a.alloc(WJ // 2, "i32").carve([512, 512])

        build_body(k, st)
        k.replay(block)
    return nc, st


def dump(k, st, name, ap, buf):
    if ("dbg_" + name) not in st["io"] or name in st["dbg_done"]:
        return
    st["dbg_done"].add(name)
    k.dma("sync", st["io"]["dbg_" + name], ap, reads=[buf], writes=[Buf("d" + name)], ch="dbg")



def load_consts(k, st):
    io, sm, smb = st["io"], st["sm"], st["smb"]
    c = {}
    for name, cols in (("dtb", DT), ("cw", DT * 4), ("cb", DT), ("ns", DT),
                       ("cmask", DT * NSL), ("onehot", DT * NSL)):
        r = sm.alloc(cols, name)
        k.dma("sync", r.ap(), io[name], writes=[r.buf], ch="init")
        c[name] = r
    cp = sm.alloc(len(CPOOL), "cpool")
    k.dma("sync", cp.ap(), io["cpool"], writes=[cp.buf], ch="init")
    for i, v in enumerate(CPOOL):
        k.nc.const_aps.register(F32, float(v), cp.ap(c0=i, cols=1))
    for eng in ("pe", "dve", "act", "gpsimd"):
        k.barrier_read(eng, [cp.buf])
    c["cpool"] = cp
    idb = smb.alloc(128, "ident")
    k.dma("sync", idb.ap(), io["ident"], writes=[idb.buf], ch="init")
    c["ident"] = idb
    onesf = sm.alloc(256, "onesf")
    k.op("dve", lambda h: h.memset(onesf.ap(), 1.0), writes=[onesf.buf])
    c["onesf"] = onesf
    onesb = smb.alloc(128, "onesb")
    k.op("dve", lambda h: h.memset(onesb.ap(), 1.0), writes=[onesb.buf])
    c["onesb"] = onesb
    mixs = sm.alloc(16, "mix")
    k.dma("sync", mixs.ap(p0=0, pn=16, cols=16), io["mix"], writes=[mixs.buf], ch="init")
    bdf = sm.alloc(128, "bdmixf")
    k.op("dve", lambda h: h.memset(bdf.ap(), 0.0), writes=[bdf.buf])
    for blk in range(8):
        dst = sm.t[16 * blk:16 * blk + 16, bdf.off + 16 * blk:bdf.off + 16 * blk + 16]
        k.dma("sync", dst, mixs.ap(p0=0, pn=16, cols=16),
              reads=[mixs.buf], writes=[bdf.buf], ch="init")
    bdb = smb.alloc(128, "bdmix")
    k.op("dve", lambda h: h.tensor_copy(out=bdb.ap(), in_=bdf.ap()),
         reads=[bdf.buf], writes=[bdb.buf])
    c["bdmix"] = bdb
    bdf.free()
    cwns = sm.alloc(DT * 4, "cwns")
    for dti in range(DT):
        k.op("dve", lambda h, dti=dti: h.tensor_scalar(
            out=cwns.ap(c0=dti * 4, cols=4), in0=c["cw"].ap(c0=dti * 4, cols=4),
            scalar1=c["ns"].ap(c0=dti, cols=1), scalar2=None, op0=ALU.mult),
            reads=[c["cw"].buf, c["ns"].buf], writes=[cwns.buf])
    c["cwns"] = cwns
    bns = sm.alloc(DT, "bns")
    nbns = sm.alloc(DT, "nbns")
    k.op("act", lambda h: h.activation(out=bns.ap(), in_=c["ns"].ap(),
                                       func=AF.Abs), reads=[c["ns"].buf],
         writes=[bns.buf])
    k.op("dve", lambda h: h.reciprocal(out=bns.ap(), in_=bns.ap()),
         reads=[bns.buf], writes=[bns.buf])
    k.op("dve", lambda h: h.tensor_scalar(out=bns.ap(), in0=bns.ap(),
                                          scalar1=60000.0, scalar2=None,
                                          op0=ALU.mult), reads=[bns.buf],
         writes=[bns.buf])
    k.op("dve", lambda h: h.tensor_scalar(out=nbns.ap(), in0=bns.ap(),
                                          scalar1=-1.0, scalar2=None,
                                          op0=ALU.mult), reads=[bns.buf],
         writes=[nbns.buf])
    c["bns"] = bns
    c["nbns"] = nbns
    return c


def wsum_stream(k, st, c, w_io, name, nch=4):
    sm, fpa = st["sm"], st["fpa"]
    acc = sm.alloc(1, name + "_acc")
    k.op("dve", lambda h: h.memset(acc.ap(), 0.0), writes=[acc.buf])
    flat = w_io.rearrange("a b -> (a b)")
    CC = 2048
    nblk = (D * H) // (128 * CC)
    bufs = [fpa.alloc(CC, f"{name}w{i}") for i in range(nch)]
    red = sm.alloc(1, name + "_red")
    for i in range(nblk):
        bb = bufs[i % nch]
        srcap = flat[i * 128 * CC:(i + 1) * 128 * CC].rearrange("(p c) -> p c", p=128)
        k.dma("sync", bb.ap(), srcap, writes=[bb.buf], ch=f"{name}s{i % nch}")
        k.op("dve", lambda h, bb=bb: h.tensor_reduce(
            out=red.ap(), in_=bb.ap(), axis=AX.X, op=ALU.add,
            apply_absolute_value=True), reads=[bb.buf], writes=[red.buf])
        k.op("dve", lambda h: h.tensor_tensor(
            out=acc.ap(), in0=acc.ap(), in1=red.ap(), op=ALU.add),
            reads=[red.buf, acc.buf], writes=[acc.buf])
    for b in bufs:
        b.free()
    red.free()
    return acc


def wsum_finish(k, st, c, acc, name):
    sm = st["sm"]
    (p0, p0b) = st["pb"][5]
    k.op("pe", lambda h: h.matmul(p0[0:1, 0:1], c["onesf"].ap(cols=1),
                                  acc.ap(), start=True, stop=True),
         reads=[c["onesf"].buf, acc.buf], writes=[p0b])
    tot = sm.alloc(1, name + "_tot")
    k.op("dve", lambda h: h.tensor_copy(out=tot.ap(p0=0, pn=1), in_=p0[0:1, 0:1]),
         reads=[p0b], writes=[tot.buf])
    k.op("dve", lambda h: h.tensor_scalar(
        out=tot.ap(p0=0, pn=1), in0=tot.ap(p0=0, pn=1),
        scalar1=1.0 / (D * H), scalar2=EPS_Q, op0=ALU.mult, op1=ALU.max),
        reads=[tot.buf], writes=[tot.buf])
    k.op("pe", lambda h: h.matmul(p0[:, 1:2], c["onesf"].ap(p0=0, pn=1, cols=128),
                                  tot.ap(p0=0, pn=1), start=True, stop=True),
         reads=[c["onesf"].buf, tot.buf], writes=[p0b])
    sc = sm.alloc(4, name)
    k.op("dve", lambda h: h.tensor_copy(out=sc.ap(cols=1), in_=p0[:, 1:2]),
         reads=[p0b], writes=[sc.buf])
    k.op("dve", lambda h: h.tensor_scalar(
        out=sc.ap(c0=1, cols=1), in0=sc.ap(cols=1), scalar1=0.5, scalar2=None,
        op0=ALU.mult), reads=[sc.buf], writes=[sc.buf])
    k.op("dve", lambda h: h.tensor_scalar(
        out=sc.ap(c0=2, cols=1), in0=sc.ap(c0=1, cols=1), scalar1=-1.0,
        scalar2=None, op0=ALU.mult), reads=[sc.buf], writes=[sc.buf])
    acc.free()
    tot.free()
    return sc


def ln_stats(k, st, c, xreg, tag):
    rw, bfa = st["rw"], st["bfa"]
    banks = [st["pb"][i] for i in (2, 3, 4, 5)]
    sqs = [st["smb"].alloc(512, tag + f"sq{i}") for i in range(2)]
    xb2s = [st["smb"].alloc(512, tag + f"xb2_{i}") for i in range(2)]
    for dti in range(DT):
        for half in range(2):
            sq = sqs[(2 * dti + half) % 2]
            xb2 = xb2s[(2 * dti + half) % 2]
            k.op("act", lambda h, dti=dti, half=half, xb2=xb2: h.activation(
                out=xb2.ap(), in_=xreg.ap(c0=dti * S + half * 512, cols=512),
                func=AF.Copy), reads=[xreg.buf], writes=[xb2.buf])
            k.op("act", lambda h, dti=dti, half=half, sq=sq: h.activation(
                out=sq.ap(), in_=xreg.ap(c0=dti * S + half * 512, cols=512),
                func=AF.Square), reads=[xreg.buf], writes=[sq.buf])

            def emit2(h, dti=dti, half=half, sq=sq, xb2=xb2):
                h.matmul(banks[half][0][0:1, 0:512],
                         c["onesb"].ap(cols=1), xb2.ap(cols=512),
                         start=(dti == 0), stop=(dti == DT - 1))
                return h.matmul(banks[2 + half][0][0:1, 0:512],
                                c["onesb"].ap(cols=1), sq.ap(cols=512),
                                start=(dti == 0), stop=(dti == DT - 1))
            k.op("pe", emit2, reads=[c["onesb"].buf, sq.buf, xb2.buf],
                 writes=[banks[half][1], banks[2 + half][1]])
    for b in sqs + xb2s:
        b.free()
    m = rw.alloc(S, tag + "m")
    var = rw.alloc(S, tag + "v")
    r = rw.alloc(S, tag + "r")
    RO = lambda reg, c0=0, n=S: reg.ap(p0=0, pn=1, c0=c0, cols=n)
    for half in range(2):
        k.op("dve", lambda h, half=half: h.tensor_scalar(
            out=RO(m, half * 512, 512), in0=banks[half][0][0:1, 0:512],
            scalar1=1.0 / D, scalar2=None, op0=ALU.mult),
            reads=[banks[half][1]], writes=[m.buf])
        k.op("dve", lambda h, half=half: h.tensor_scalar(
            out=RO(var, half * 512, 512), in0=banks[2 + half][0][0:1, 0:512],
            scalar1=1.0 / D, scalar2=None, op0=ALU.mult),
            reads=[banks[2 + half][1]], writes=[var.buf])
    mm = rw.alloc(S, tag + "mm")
    k.op("dve", lambda h: h.tensor_tensor(out=RO(mm), in0=RO(m), in1=RO(m),
                                          op=ALU.mult), reads=[m.buf], writes=[mm.buf])
    k.op("dve", lambda h: h.tensor_tensor(out=RO(var), in0=RO(var), in1=RO(mm),
                                          op=ALU.subtract),
         reads=[var.buf, mm.buf], writes=[var.buf])
    mm.free()
    k.op("act", lambda h: h.activation(out=RO(r), in_=RO(var), func=AF.Ln,
                                       bias=EPS_LN), reads=[var.buf], writes=[r.buf])
    k.op("act", lambda h: h.activation(out=RO(r), in_=RO(r), func=AF.Exp,
                                       scale=-0.5), reads=[r.buf], writes=[r.buf])
    return {"m": m, "var": var, "r": r}


def bcast_rows(k, st, c, rows, tag):
    out = []
    bank_ids = [(2, 3), (4, 5)]
    for i, reg in enumerate(rows):
        (bA, bAb) = st["pb"][bank_ids[i][0]]
        (bB, bBb) = st["pb"][bank_ids[i][1]]

        def emit(h, reg=reg, bA=bA, bB=bB):
            h.matmul(bA[:, 0:512], c["onesf"].ap(p0=0, pn=1, cols=128),
                     reg.ap(p0=0, pn=1, cols=512), start=True, stop=True)
            return h.matmul(bB[:, 0:512], c["onesf"].ap(p0=0, pn=1, cols=128),
                            reg.ap(p0=0, pn=1, c0=512, cols=512), start=True, stop=True)
        k.op("pe", emit, reads=[reg.buf, c["onesf"].buf], writes=[bAb, bBb])

        def getter(n, bA=bA, bB=bB):
            return (bA if n == 0 else bB)[:, 0:512]
        out.append((getter, [bAb, bBb]))
    return out


def ternary_quant(k, st, sc, wreg, qap_fn, qbuf, tag):
    fpa = st["fpa"]
    ga = fpa.alloc(wreg.cols, tag + "a")
    gb = fpa.alloc(wreg.cols, tag + "b")
    k.op("dve", lambda h: h.tensor_scalar(
        out=ga.ap(), in0=wreg.ap(), scalar1=sc.ap(c0=1, cols=1), scalar2=None,
        op0=ALU.is_ge), reads=[wreg.buf, sc.buf], writes=[ga.buf])
    k.op("dve", lambda h: h.tensor_scalar(
        out=gb.ap(), in0=wreg.ap(), scalar1=sc.ap(c0=2, cols=1), scalar2=None,
        op0=ALU.is_le), reads=[wreg.buf, sc.buf], writes=[gb.buf])
    k.op("dve", lambda h: h.tensor_tensor(
        out=qap_fn(), in0=ga.ap(), in1=gb.ap(), op=ALU.subtract),
        reads=[ga.buf, gb.buf], writes=[qbuf])
    ga.free()
    gb.free()


def ffn_strip(k, st, c, sc1, sc2, strip, jit):
    io, scr, scrbuf = st["io"], st["scr"], st["scrbuf"]
    fpa, bfa, sm = st["fpa"], st["bfa"], st["sm"]
    xres = st["xres"]
    tg = f"f{strip}"

    stats = ln_stats(k, st, c, xres, tg)
    xbf = bfa.alloc(DT * S, tg + "xbf")
    tmp = fpa.alloc(S, tg + "xtmp")
    bc = bcast_rows(k, st, c, [stats["m"], stats["r"]], tg + "bc")
    for dti in range(DT):
        for n in range(2):
            k.op("dve", lambda h, dti=dti, n=n: h.tensor_tensor(
                out=tmp.ap(c0=n * 512, cols=512),
                in0=xres.ap(c0=dti * S + n * 512, cols=512),
                in1=bc[0][0](n), op=ALU.subtract),
                reads=[xres.buf] + bc[0][1], writes=[tmp.buf])
            k.op("dve", lambda h, dti=dti, n=n: h.tensor_tensor(
                out=xbf.ap(c0=dti * S + n * 512, cols=512),
                in0=tmp.ap(c0=n * 512, cols=512),
                in1=bc[1][0](n), op=ALU.mult),
                reads=[tmp.buf] + bc[1][1], writes=[xbf.buf])
    tmp.free()
    stats["m"].free(); stats["var"].free(); stats["r"].free()

    stg = [fpa.alloc(256, tg + f"st{i}") for i in range(2)]
    rmxa = sm.alloc(TPS * 32, tg + "rmxa")
    q1c = [bfa.alloc(DT * 256, tg + f"q1_{i}") for i in range(2)]
    wst = [fpa.alloc(DT * 256, tg + "ws_0")] if jit else None

    for w in range(NW):
        for jc in range(8):
            cidx = w * 8 + jc
            qb = q1c[cidx % 2]
            if jit:
                wb = wst[0]
                k.dma("sync", wb.ap(),
                      io["w1"][:, cidx * DT * 256:(cidx + 1) * DT * 256],
                      writes=[wb.buf], ch="w1s")
                for hh in range(2):
                    wh = Region(wb.arena, wb.off + hh * 2048, 2048, wb.buf)
                    ternary_quant(
                        k, st, sc1, wh,
                        lambda hh=hh, qb=qb: qb.ap(c0=hh * 2048, cols=2048),
                        qb.buf, tg + "q1")
                k.dma("act", scr["q1"].ap()[cidx], qb.ap(),
                      reads=[qb.buf], writes=[scrbuf["q1"]], ch="q1st")
            else:
                k.dma("sync", qb.ap(), scr["q1"].ap()[cidx],
                      reads=[scrbuf["q1"]], writes=[qb.buf], ch=f"q1l{cidx % 2}")
            for t in range(TPS):
                (ph, phb) = st["pb"][t % 2]
                sg = stg[t % 2]

                def emitm(h, t=t, qb=qb, ph=ph):
                    last = None
                    for dti in range(DT):
                        last = h.matmul(ph[:, 0:256],
                                        xbf.ap(c0=dti * S + t * 128, cols=128),
                                        qb.ap(c0=dti * 256, cols=256),
                                        start=(dti == 0), stop=(dti == DT - 1))
                    return last
                k.op("pe", emitm, reads=[xbf.buf, qb.buf], writes=[phb])
                k.op("act", lambda h, ph=ph, sg=sg: h.activation(
                    out=sg.ap(), in_=ph[:, 0:256],
                    func=AF.Relu, scale=sc1.ap(cols=1)),
                    reads=[phb, sc1.buf], writes=[sg.buf])
                k.dma("act", scr["reluh"].ap()[w, t, :, jc * 256:(jc + 1) * 256],
                      sg.ap(), reads=[sg.buf], writes=[scrbuf["reluh"]], ch=f"rhs{t % 2}")
                k.op("dve", lambda h, sg=sg, t=t, cidx=cidx: h.reduce_max(
                    out=rmxa.ap(c0=t * 32 + cidx, cols=1), in_=sg.ap(), axis=AX.X),
                    reads=[sg.buf], writes=[rmxa.buf])
    for b in q1c:
        b.free()
    if wst:
        for b in wst:
            b.free()
    xbf.free()
    for b in stg:
        b.free()
    rmx = sm.alloc(TPS, tg + "rmx")
    for t in range(TPS):
        k.op("dve", lambda h, t=t: h.reduce_max(
            out=rmx.ap(c0=t, cols=1), in_=rmxa.ap(c0=t * 32, cols=32), axis=AX.X),
            reads=[rmxa.buf], writes=[rmx.buf])
    rmxa.free()

    xm = sm.alloc(TPS, tg + "xm")
    squ = sm.alloc(TPS, tg + "squ")
    sh = sm.alloc(TPS, tg + "sh")
    k.op("dve", lambda h: h.tensor_tensor(out=xm.ap(), in0=rmx.ap(), in1=rmx.ap(),
                                          op=ALU.mult), reads=[rmx.buf], writes=[xm.buf])
    k.op("dve", lambda h: h.tensor_scalar(out=xm.ap(), in0=xm.ap(), scalar1=EPS_Q,
                                          scalar2=None, op0=ALU.max),
         reads=[xm.buf], writes=[xm.buf])
    k.op("dve", lambda h: h.reciprocal(out=squ.ap(), in_=xm.ap()),
         reads=[xm.buf], writes=[squ.buf])
    k.op("dve", lambda h: h.tensor_scalar(out=squ.ap(), in0=squ.ap(), scalar1=255.0,
                                          scalar2=None, op0=ALU.mult),
         reads=[squ.buf], writes=[squ.buf])
    k.op("act", lambda h: h.activation(out=squ.ap(), in_=squ.ap(), func=AF.Ln),
         reads=[squ.buf], writes=[squ.buf])
    k.op("act", lambda h: h.activation(out=squ.ap(), in_=squ.ap(), func=AF.Exp,
                                       scale=0.5), reads=[squ.buf], writes=[squ.buf])
    k.op("dve", lambda h: h.tensor_scalar(out=sh.ap(), in0=xm.ap(),
                                          scalar1=1.0 / 255.0, scalar2=None,
                                          op0=ALU.mult), reads=[xm.buf], writes=[sh.buf])
    rmx.free()
    xm.free()
    dsh = st["smb"].alloc(TPS * 128, tg + "dsh")
    for t in range(TPS):
        k.op("dve", lambda h, t=t: h.tensor_scalar(
            out=dsh.ap(c0=t * 128, cols=128), in0=c["ident"].ap(),
            scalar1=sh.ap(c0=t, cols=1), scalar2=None, op0=ALU.mult),
            reads=[c["ident"].buf, sh.buf], writes=[dsh.buf])
    sh.free()

    if jit and "sc2_fix" in st:
        st["sc2_fix"]()
        del st["sc2_fix"]
    hqT = bfa.alloc(DT * S, tg + "hqT")
    q2c = [bfa.alloc(DT * 128, tg + f"q2_{i}") for i in range(2)]
    w2st = fpa.alloc(DT * 128, tg + "w2s") if jit else None

    for w in range(NW):
        if jit:
            for jl in range(DT):
                jt = w * DT + jl
                src = io["w2t"][jt * 128:(jt + 1) * 128, :]
                k.dma("sync", w2st.ap(), src, writes=[w2st.buf], ch="w2s")
                qq = bfa.alloc(DT * 128, tg + "qq2")
                ternary_quant(k, st, sc2, w2st, lambda qq=qq: qq.ap(), qq.buf,
                              tg + "q2")
                k.dma("act", scr["q2t"].ap()[jt], qq.ap(),
                      reads=[qq.buf], writes=[scrbuf["q2t"]], ch="q2st")
                qq.free()
        i32c = st["i32c"]
        rbs = [fpa.alloc(WJ, tg + f"rb{i}") for i in range(2)]
        ubs = [fpa.alloc(WJ, tg + f"u{i}") for i in range(2)]
        fbs = [fpa.alloc(WJ, tg + f"f{i}") for i in range(2)]
        rbfs = [bfa.alloc(WJ, tg + f"rq{i}") for i in range(2)]
        for t in range(TPS):
            rbb = rbs[t % 2]
            ub = ubs[t % 2]
            fb = fbs[t % 2]
            rbf = rbfs[t % 2]
            k.dma("sync", rbb.ap(), scr["reluh"].ap()[w, t],
                  reads=[scrbuf["reluh"]], writes=[rbb.buf], ch=f"rhl{t % 2}")
            k.op("act", lambda h, rbb=rbb, t=t, ub=ub: h.activation(
                out=ub.ap(), in_=rbb.ap(), func=AF.Square,
                scale=squ.ap(c0=t, cols=1)),
                reads=[rbb.buf, squ.buf], writes=[ub.buf])
            i3 = i32c[t % 2]
            for hf in range(4):
                k.op("dve", lambda h, ub=ub, hf=hf, i3=i3: h.tensor_copy(
                    out=i3.arena.t[:, i3.off:i3.off + 512],
                    in_=ub.ap(c0=hf * 512, cols=512)),
                    reads=[ub.buf], writes=[i3.buf])
                k.op("dve", lambda h, fb=fb, hf=hf, i3=i3: h.tensor_copy(
                    out=fb.ap(c0=hf * 512, cols=512),
                    in_=i3.arena.t[:, i3.off:i3.off + 512]),
                    reads=[i3.buf], writes=[fb.buf])
            k.op("dve", lambda h, ub=ub, fb=fb: h.tensor_tensor(
                out=ub.ap(), in0=ub.ap(), in1=fb.ap(), op=ALU.subtract),
                reads=[ub.buf, fb.buf], writes=[ub.buf])
            k.op("dve", lambda h, ub=ub: h.tensor_scalar(
                out=ub.ap(), in0=ub.ap(), scalar1=0.5, scalar2=None,
                op0=ALU.is_ge), reads=[ub.buf], writes=[ub.buf])
            k.op("dve", lambda h, ub=ub, fb=fb, rbf=rbf: h.tensor_tensor(
                out=rbf.ap(), in0=fb.ap(), in1=ub.ap(), op=ALU.add),
                reads=[fb.buf, ub.buf], writes=[rbf.buf])
            for g in range(4):
                (pT, pTb) = st["pb"][2 + (g % 2)]

                def emitt(h, g=g, t=t, pT=pT, rbf=rbf):
                    last = None
                    for jj in range(4):
                        jl = g * 4 + jj
                        last = h.matmul(pT[:, jj * 128:(jj + 1) * 128],
                                        rbf.ap(c0=jl * 128, cols=128),
                                        dsh.ap(c0=t * 128, cols=128),
                                        start=True, stop=True)
                    return last
                k.op("pe", emitt, reads=[rbf.buf, dsh.buf], writes=[pTb])
                k.op("act", lambda h, g=g, t=t, pT=pT: h.activation(
                    out=hqT.ap().rearrange("p (j c) -> p j c", j=DT)
                    [:, g * 4:(g + 1) * 4, t * 128:(t + 1) * 128],
                    in_=pT[:, 0:512].rearrange("p (j c) -> p j c", j=4),
                    func=AF.Copy),
                    reads=[pTb], writes=[hqT.buf])
        for b in rbs + ubs + fbs + rbfs:
            b.free()
        for dti in range(DT):
            qb = q2c[dti % 2]
            for g4 in range(4):
                k.dma("sync",
                      qb.ap(c0=g4 * 4 * 128, cols=4 * 128)
                      .rearrange("p (j c) -> p j c", j=4),
                      scr["q2t"].ap()[w * DT + g4 * 4:w * DT + (g4 + 1) * 4, :,
                                      dti * 128:(dti + 1) * 128]
                      .rearrange("j p c -> p j c"),
                      reads=[scrbuf["q2t"]], writes=[qb.buf], ch=f"q2l{g4}")
            for n in range(2):
                (po, pob) = st["pb"][2 * (dti % 2) + n]

                def emit2(h, qb=qb, n=n, po=po):
                    last = None
                    for jl in range(DT):
                        last = h.matmul(po[:, 0:512],
                                        qb.ap(c0=jl * 128, cols=128),
                                        hqT.ap(c0=jl * 1024 + n * 512, cols=512),
                                        start=(jl == 0), stop=(jl == DT - 1))
                    return last
                k.op("pe", emit2, reads=[qb.buf, hqT.buf], writes=[pob])
                k.op("dve", lambda h, dti=dti, n=n, po=po: h.scalar_tensor_tensor(
                    out=xres.ap(c0=dti * S + n * 512, cols=512), in0=po[:, 0:512],
                    scalar=sc2.ap(c0=3, cols=1),
                    in1=xres.ap(c0=dti * S + n * 512, cols=512),
                    op0=ALU.mult, op1=ALU.add),
                    reads=[pob, sc2.buf, xres.buf], writes=[xres.buf])
    for b in q2c:
        b.free()
    if w2st:
        w2st.free()
    hqT.free(); dsh.free(); squ.free()


def attn_strip(k, st, c, strip, own):
    io, scr, scrbuf = st["io"], st["scr"], st["scrbuf"]
    fpa, bfa, sm, rw = st["fpa"], st["bfa"], st["sm"], st["rw"]
    xres = st["xres"]
    tg = f"a{strip}"
    jit = (strip == 0)

    stats = ln_stats(k, st, c, xres, tg)
    RO = lambda reg, c0=0, n=S: reg.ap(p0=0, pn=1, c0=c0, cols=n)
    Crow = rw.alloc(S, tg + "C")
    k.op("dve", lambda h: h.tensor_tensor(out=RO(Crow), in0=RO(stats["r"]),
                                          in1=RO(stats["r"]), op=ALU.mult),
         reads=[stats["r"].buf], writes=[Crow.buf])
    k.op("dve", lambda h: h.tensor_tensor(out=RO(Crow), in0=RO(Crow),
                                          in1=RO(stats["var"]), op=ALU.mult),
         reads=[Crow.buf, stats["var"].buf], writes=[Crow.buf])
    stats["var"].free()
    k.op("act", lambda h: h.activation(out=RO(Crow), in_=RO(Crow), func=AF.Ln,
                                       bias=EPS_RMS), reads=[Crow.buf], writes=[Crow.buf])
    k.op("act", lambda h: h.activation(out=RO(Crow), in_=RO(Crow), func=AF.Exp,
                                       scale=-0.5), reads=[Crow.buf], writes=[Crow.buf])
    k.op("dve", lambda h: h.tensor_tensor(out=RO(Crow), in0=RO(Crow),
                                          in1=RO(stats["r"]), op=ALU.mult),
         reads=[Crow.buf, stats["r"].buf], writes=[Crow.buf])
    D2row = rw.alloc(S, tg + "D")
    k.op("dve", lambda h: h.tensor_tensor(out=RO(D2row), in0=RO(stats["m"]),
                                          in1=RO(Crow), op=ALU.mult),
         reads=[stats["m"].buf, Crow.buf], writes=[D2row.buf])
    k.op("dve", lambda h: h.tensor_scalar(out=RO(D2row), in0=RO(D2row),
                                          scalar1=-1.0, scalar2=None, op0=ALU.mult),
         reads=[D2row.buf], writes=[D2row.buf])
    bc = bcast_rows(k, st, c, [Crow, D2row], tg + "bc")
    CbS = fpa.alloc(S, tg + "CbS")
    DbS = fpa.alloc(S, tg + "DbS")
    for n in range(2):
        k.op("act", lambda h, n=n: h.activation(
            out=CbS.ap(c0=n * 512, cols=512), in_=bc[0][0](n), func=AF.Copy),
            reads=bc[0][1], writes=[CbS.buf])
        k.op("act", lambda h, n=n: h.activation(
            out=DbS.ap(c0=n * 512, cols=512), in_=bc[1][0](n), func=AF.Copy),
            reads=bc[1][1], writes=[DbS.buf])
    Crow.free()
    D2row.free()
    if own:
        Arow = rw.alloc(S, "A3")
        Brow = rw.alloc(S, "B3")
        k.op("dve", lambda h: h.tensor_scalar(out=RO(Arow), in0=RO(stats["r"]),
                                              scalar1=1.0, scalar2=None, op0=ALU.add),
             reads=[stats["r"].buf], writes=[Arow.buf])
        k.op("dve", lambda h: h.tensor_tensor(out=RO(Brow), in0=RO(stats["m"]),
                                              in1=RO(stats["r"]), op=ALU.mult),
             reads=[stats["m"].buf, stats["r"].buf], writes=[Brow.buf])
        k.op("dve", lambda h: h.tensor_scalar(out=RO(Brow), in0=RO(Brow),
                                              scalar1=-1.0, scalar2=None, op0=ALU.mult),
             reads=[Brow.buf], writes=[Brow.buf])
        st["Arow"] = Arow
        st["Brow"] = Brow
    stats["m"].free(); stats["r"].free()
    xc = bfa.alloc(DT * S, tg + "xc")
    xnbs = [fpa.alloc(S + 64, tg + f"xn{i}") for i in range(2)]
    caccs = [fpa.alloc(S, tg + f"cacc{i}") for i in range(2)]
    if "halo" not in st:
        st["halo"] = sm.alloc(DT * 3, "halo")
        k.op("dve", lambda h: h.memset(st["halo"].ap(), 0.0), writes=[st["halo"].buf])
    halo = st["halo"]
    for dti in range(DT):
        xnb = xnbs[dti % 2]
        cacc = caccs[dti % 2]
        k.op("dve", lambda h, dti=dti, xnb=xnb: h.tensor_copy(
            out=xnb.ap(c0=0, cols=3), in_=halo.ap(c0=dti * 3, cols=3)),
            reads=[halo.buf], writes=[xnb.buf])
        k.op("gpsimd", lambda h, dti=dti, xnb=xnb: h.tensor_tensor(
            out=xnb.ap(c0=3, cols=S), in0=xres.ap(c0=dti * S, cols=S),
            in1=CbS.ap(), op=ALU.mult),
            reads=[xres.buf, CbS.buf], writes=[xnb.buf])
        k.op("gpsimd", lambda h, dti=dti, xnb=xnb: h.tensor_tensor(
            out=xnb.ap(c0=3, cols=S), in0=xnb.ap(c0=3, cols=S),
            in1=DbS.ap(), op=ALU.add),
            reads=[xnb.buf, DbS.buf], writes=[xnb.buf])
        k.op("dve", lambda h, dti=dti, xnb=xnb: h.tensor_scalar(
            out=xnb.ap(c0=3, cols=S), in0=xnb.ap(c0=3, cols=S),
            scalar1=c["bns"].ap(c0=dti, cols=1),
            scalar2=c["nbns"].ap(c0=dti, cols=1),
            op0=ALU.min, op1=ALU.max),
            reads=[xnb.buf, c["bns"].buf, c["nbns"].buf], writes=[xnb.buf])
        k.op("dve", lambda h, dti=dti, xnb=xnb: h.tensor_copy(
            out=halo.ap(c0=dti * 3, cols=3), in_=xnb.ap(c0=S, cols=3)),
            reads=[xnb.buf], writes=[halo.buf])
        k.op("dve", lambda h, dti=dti, xnb=xnb, cacc=cacc: h.tensor_scalar(
            out=cacc.ap(), in0=xnb.ap(c0=0, cols=S),
            scalar1=c["cwns"].ap(c0=dti * 4, cols=1), scalar2=None, op0=ALU.mult),
            reads=[xnb.buf, c["cwns"].buf], writes=[cacc.buf])
        for kk in range(1, 4):
            k.op("dve", lambda h, dti=dti, kk=kk, xnb=xnb, cacc=cacc:
                 h.scalar_tensor_tensor(
                     out=cacc.ap(), in0=xnb.ap(c0=kk, cols=S),
                     scalar=c["cwns"].ap(c0=dti * 4 + kk, cols=1),
                     in1=cacc.ap(), op0=ALU.mult, op1=ALU.add),
                 reads=[xnb.buf, c["cwns"].buf, cacc.buf], writes=[cacc.buf])
        k.op("act", lambda h, dti=dti, cacc=cacc: h.activation(
            out=xc.ap(c0=dti * S, cols=S), in_=cacc.ap(), func=AF.Silu,
            bias=c["cb"].ap(c0=dti, cols=1)),
            reads=[cacc.buf, c["cb"].buf], writes=[xc.buf])
    for b in xnbs + caccs:
        b.free()
    CbS.free()
    DbS.free()

    if "bdb" not in st:
        st["bdb"] = sm.alloc(DT * NSL, "bdb")
        st["bob"] = sm.alloc(DT * NSL, "bob")
    bdb, bob = st["bdb"], st["bob"]

    winb = [bfa.alloc(DT * 128, tg + f"win{i}") for i in range(3)]
    wfs = fpa.alloc(DT * 128, tg + "wfs") if (jit or own) else None
    ropeb = fpa.alloc(2048, tg + "rope")
    cosb, sinb = ropeb.carve([1024, 1024])
    dtb2 = fpa.alloc(2048, tg + "dt")
    dtt = dtb2.carve([1024, 1024])
    vinb = fpa.alloc(2048, tg + "vin")
    vint = vinb.carve([1024, 1024])
    mcco = fpa.alloc(2048, tg + "mcco")
    mcb, cob = mcco.carve([1024, 1024])
    sscr = fpa.alloc(2048, tg + "ssc")
    sA, sB, sC, sD = sscr.carve([512, 512, 512, 512])
    nm0s = sm.alloc(4, tg + "nm0")
    wn = [0]

    def load_win(jt, direct=False):
        wb = winb[wn[0] % 3]
        wn[0] += 1
        if jit or direct:
            k.dma("sync", wfs.ap(), io["wint"][:, jt * DT * 128:(jt + 1) * DT * 128],
                  writes=[wfs.buf], ch="wis")
            k.op("act", lambda h, wb=wb: h.activation(out=wb.ap(), in_=wfs.ap(),
                                                      func=AF.Copy),
                 reads=[wfs.buf], writes=[wb.buf])
            if not direct:
                k.dma("act", scr["winbf"].ap()[jt], wb.ap(),
                      reads=[wb.buf], writes=[scrbuf["winbf"]], ch="wist")
        else:
            k.dma("sync", wb.ap(), scr["winbf"].ap()[jt],
                  reads=[scrbuf["winbf"]], writes=[wb.buf], ch=f"winl{wn[0] % 3}")
        return wb

    def mm_tile(jt, wb, pb_pair):
        for n in range(2):
            (po, pob) = pb_pair[n]

            def emit(h, wb=wb, n=n, po=po):
                last = None
                for dti in range(DT):
                    last = h.matmul(po[:, 0:512],
                                    wb.ap(c0=dti * 128, cols=128),
                                    xc.ap(c0=dti * S + n * 512, cols=512),
                                    start=(dti == 0), stop=(dti == DT - 1))
                return last
            k.op("pe", emit, reads=[wb.buf, xc.buf], writes=[pob])

    for pair in range(8):
        lo, hi = pair, pair + 8
        for idx, tile in enumerate((lo, hi)):
            wb = load_win(tile)
            mm_tile(tile, wb, [st["pb"][0], st["pb"][1]])
            for n in range(2):
                (po, pob) = st["pb"][n]
                k.op("act", lambda h, po=po, tile=tile: h.activation(
                    out=sA.ap(), in_=po[:, 0:512], func=AF.Exp,
                    bias=c["dtb"].ap(c0=tile, cols=1)),
                    reads=[pob, c["dtb"].buf], writes=[sA.buf])
                k.op("act", lambda h: h.activation(
                    out=sB.ap(), in_=sA.ap(), func=AF.Ln, bias=1.0),
                    reads=[sA.buf], writes=[sB.buf])
                k.op("dve", lambda h, idx=idx, n=n: h.tensor_scalar(
                    out=dtt[idx].ap(c0=n * 512, cols=512), in0=sB.ap(),
                    scalar1=0.001, scalar2=2.0, op0=ALU.max, op1=ALU.min),
                    reads=[sB.buf], writes=[dtt[idx].buf])
        k.dma("sync", cosb.ap(),
              io["costab"][pair * 128:(pair + 1) * 128, strip * S:(strip + 1) * S],
              writes=[cosb.buf], ch="rope0")
        k.dma("sync", sinb.ap(),
              io["sintab"][pair * 128:(pair + 1) * 128, strip * S:(strip + 1) * S],
              writes=[sinb.buf], ch="rope1")
        wb = load_win(16 + lo)
        mm_tile(16 + lo, wb, [st["pb"][0], st["pb"][1]])
        wb = load_win(16 + hi)
        mm_tile(16 + hi, wb, [st["pb"][2], st["pb"][3]])
        for n in range(2):
            (p1, p1b) = st["pb"][n]
            (p2, p2b) = st["pb"][2 + n]
            k.op("dve", lambda h, p1=p1, n=n: h.tensor_tensor(
                out=sA.ap(), in0=p1[:, 0:512], in1=cosb.ap(c0=n * 512, cols=512),
                op=ALU.mult), reads=[p1b, cosb.buf], writes=[sA.buf])
            k.op("dve", lambda h, p2=p2, n=n: h.tensor_tensor(
                out=sB.ap(), in0=p2[:, 0:512], in1=sinb.ap(c0=n * 512, cols=512),
                op=ALU.mult), reads=[p2b, sinb.buf], writes=[sB.buf])
            k.op("dve", lambda h, p1=p1, n=n: h.tensor_tensor(
                out=sC.ap(), in0=p1[:, 0:512], in1=sinb.ap(c0=n * 512, cols=512),
                op=ALU.mult), reads=[p1b, sinb.buf], writes=[sC.buf])
            k.op("dve", lambda h, p2=p2, n=n: h.tensor_tensor(
                out=sD.ap(), in0=p2[:, 0:512], in1=cosb.ap(c0=n * 512, cols=512),
                op=ALU.mult), reads=[p2b, cosb.buf], writes=[sD.buf])
            k.op("gpsimd", lambda h: h.tensor_tensor(
                out=sA.ap(), in0=sA.ap(), in1=sB.ap(), op=ALU.subtract),
                reads=[sA.buf, sB.buf], writes=[sA.buf])
            k.op("gpsimd", lambda h: h.tensor_tensor(
                out=sC.ap(), in0=sC.ap(), in1=sD.ap(), op=ALU.add),
                reads=[sC.buf, sD.buf], writes=[sC.buf])
            k.op("dve", lambda h, n=n: h.tensor_tensor(
                out=vint[0].ap(c0=n * 512, cols=512), in0=sA.ap(),
                in1=dtt[0].ap(c0=n * 512, cols=512), op=ALU.mult),
                reads=[sA.buf, dtt[0].buf], writes=[vint[0].buf])
            k.op("gpsimd", lambda h, n=n: h.tensor_tensor(
                out=vint[1].ap(c0=n * 512, cols=512), in0=sC.ap(),
                in1=dtt[1].ap(c0=n * 512, cols=512), op=ALU.mult),
                reads=[sC.buf, dtt[1].buf], writes=[vint[1].buf])
        if own:
            for idx, tile in enumerate((lo, hi)):
                wb = load_win(32 + tile, direct=True)
                mm_tile(32 + tile, wb, [st["pb"][4], st["pb"][5]])
                for n in range(2):
                    (po, pob) = st["pb"][4 + n]
                    k.op("act", lambda h, po=po: h.activation(
                        out=sA.ap(), in_=po[:, 0:512],
                        func=AF.Tanh, scale=0.5), reads=[pob], writes=[sA.buf])
                    dst = (sC, sD)[n]
                    k.op("dve", lambda h, dst=dst: h.tensor_scalar(
                        out=dst.ap(), in0=sA.ap(),
                        scalar1=0.5, scalar2=0.5, op0=ALU.mult, op1=ALU.add),
                        reads=[sA.buf], writes=[dst.buf])
                k.dma("act", scr["g"].ap()[tile], sscr.ap(c0=1024, cols=1024),
                      reads=[sC.buf, sD.buf], writes=[scrbuf["g"]], ch="gst")
        for idx, tile in enumerate((lo, hi)):
            dtt_i, vin_i = dtt[idx], vint[idx]
            for cc in range(4):
                k.op("dve", lambda h, cc=cc, dtt_i=dtt_i: h.tensor_tensor_scan(
                    out=mcb.ap(c0=cc * 256, cols=256),
                    data0=c["onesf"].ap(cols=256),
                    data1=dtt_i.ap(c0=cc * 256, cols=256),
                    initial=0.0, op0=ALU.mult, op1=ALU.add),
                    reads=[c["onesf"].buf, dtt_i.buf], writes=[mcb.buf])
            k.op("dve", lambda h: h.tensor_scalar(
                out=mcb.ap(), in0=mcb.ap(), scalar1=20.0, scalar2=None,
                op0=ALU.min), reads=[mcb.buf], writes=[mcb.buf])
            for cc in range(4):
                k.op("dve", lambda h, cc=cc: h.tensor_scalar(
                    out=nm0s.ap(c0=cc, cols=1), in0=mcb.ap(c0=cc * 256, cols=1),
                    scalar1=-1.0, scalar2=None, op0=ALU.mult),
                    reads=[mcb.buf], writes=[nm0s.buf])
                dst = (sA, sB)[cc // 2]
                k.op("act", lambda h, cc=cc, dst=dst: h.activation(
                    out=dst.ap(c0=(cc % 2) * 256, cols=256),
                    in_=mcb.ap(c0=cc * 256, cols=256),
                    func=AF.Exp, bias=nm0s.ap(c0=cc, cols=1)),
                    reads=[mcb.buf, nm0s.buf], writes=[dst.buf])
            k.op("gpsimd", lambda h, vin_i=vin_i: h.tensor_tensor(
                out=sC.ap(), in0=sA.ap(), in1=vin_i.ap(cols=512), op=ALU.mult),
                reads=[sA.buf, vin_i.buf], writes=[sC.buf])
            k.op("gpsimd", lambda h, vin_i=vin_i: h.tensor_tensor(
                out=sD.ap(), in0=sB.ap(), in1=vin_i.ap(c0=512, cols=512),
                op=ALU.mult), reads=[sB.buf, vin_i.buf], writes=[sD.buf])
            for cc in range(4):
                zsrc = (sC, sD)[cc // 2]
                dst = (sA, sB)[cc // 2]
                k.op("dve", lambda h, cc=cc, zsrc=zsrc, dst=dst: h.tensor_tensor_scan(
                    out=dst.ap(c0=(cc % 2) * 256, cols=256),
                    data0=c["onesf"].ap(cols=256),
                    data1=zsrc.ap(c0=(cc % 2) * 256, cols=256),
                    initial=0.0, op0=ALU.mult, op1=ALU.add),
                    reads=[c["onesf"].buf, zsrc.buf], writes=[dst.buf])
            for cc in range(4):
                dst = (sC, sD)[cc // 2]
                k.op("act", lambda h, cc=cc, dst=dst: h.activation(
                    out=dst.ap(c0=(cc % 2) * 256, cols=256),
                    in_=mcb.ap(c0=cc * 256, cols=256),
                    func=AF.Exp, scale=-1.0, bias=mcb.ap(c0=cc * 256, cols=1)),
                    reads=[mcb.buf], writes=[dst.buf])
            k.op("gpsimd", lambda h: h.tensor_tensor(
                out=cob.ap(cols=512), in0=sC.ap(), in1=sA.ap(), op=ALU.mult),
                reads=[sC.buf, sA.buf], writes=[cob.buf])
            k.op("gpsimd", lambda h: h.tensor_tensor(
                out=cob.ap(c0=512, cols=512), in0=sD.ap(), in1=sB.ap(),
                op=ALU.mult), reads=[sD.buf, sB.buf], writes=[cob.buf])
            for cc in range(4):
                slot = strip * 4 + cc
                k.op("dve", lambda h, tile=tile, slot=slot, cc=cc: h.tensor_scalar(
                    out=bdb.ap(c0=tile * NSL + slot, cols=1),
                    in0=mcb.ap(c0=cc * 256 + 255, cols=1),
                    scalar1=-1.0, scalar2=None, op0=ALU.mult),
                    reads=[mcb.buf], writes=[bdb.buf])
                k.op("dve", lambda h, tile=tile, slot=slot, cc=cc: h.tensor_copy(
                    out=bob.ap(c0=tile * NSL + slot, cols=1),
                    in_=cob.ap(c0=cc * 256 + 255, cols=1)),
                    reads=[cob.buf], writes=[bob.buf])
            if own:
                k.dma("act", scr["mc"].ap()[tile], mcb.ap(),
                      reads=[mcb.buf], writes=[scrbuf["mc"]], ch="mcst")
                k.dma("act", scr["co"].ap()[tile], cob.ap(),
                      reads=[cob.buf], writes=[scrbuf["co"]], ch="cost")
    
    for b in winb:
        b.free()
    if wfs:
        wfs.free()
    ropeb.free(); dtb2.free(); vinb.free(); mcco.free(); sscr.free()
    nm0s.free()
    xc.free()


def carry_and_finish(k, st, c):
    io, scr, scrbuf = st["io"], st["scr"], st["scrbuf"]
    fpa, bfa, sm = st["fpa"], st["bfa"], st["sm"]
    xres = st["xres"]
    bdb, bob = st["bdb"], st["bob"]

    carries = sm.alloc(DT * NSL, "carries")
    scr2 = sm.alloc(6 * NSL, "cscratch")
    k.op("dve", lambda h: h.tensor_tensor(out=bdb.ap(), in0=bdb.ap(),
                                          in1=c["cmask"].ap(), op=ALU.mult),
         reads=[bdb.buf, c["cmask"].buf], writes=[bdb.buf])
    k.op("dve", lambda h: h.tensor_tensor(out=bob.ap(), in0=bob.ap(),
                                          in1=c["cmask"].ap(), op=ALU.mult),
         reads=[bob.buf, c["cmask"].buf], writes=[bob.buf])
    for jt in range(DT):
        o = jt * NSL
        A = lambda c0, n=NSL: scr2.ap(c0=c0 * NSL, cols=n)
        k.op("dve", lambda h, o=o: h.tensor_tensor_scan(
            out=A(0), data0=c["onesf"].ap(cols=NSL), data1=bdb.ap(c0=o, cols=NSL),
            initial=0.0, op0=ALU.mult, op1=ALU.add),
            reads=[c["onesf"].buf, bdb.buf], writes=[scr2.buf])
        k.op("dve", lambda h: h.tensor_scalar(
            out=A(0), in0=A(0), scalar1=-80.0, scalar2=None, op0=ALU.max),
            reads=[scr2.buf], writes=[scr2.buf])
        k.op("dve", lambda h, o=o: h.tensor_tensor(
            out=A(1), in0=A(0), in1=c["onehot"].ap(c0=o, cols=NSL), op=ALU.mult),
            reads=[scr2.buf, c["onehot"].buf], writes=[scr2.buf])
        k.op("dve", lambda h: h.tensor_reduce(
            out=A(2, 1), in_=A(1), axis=AX.X, op=ALU.add),
            reads=[scr2.buf], writes=[scr2.buf])
        k.op("dve", lambda h: h.tensor_scalar(
            out=A(1), in0=A(0), scalar1=A(2, 1), scalar2=-20.0,
            op0=ALU.subtract, op1=ALU.max),
            reads=[scr2.buf], writes=[scr2.buf])
        k.op("dve", lambda h: h.tensor_scalar(
            out=A(1), in0=A(1), scalar1=0.0, scalar2=None, op0=ALU.min),
            reads=[scr2.buf], writes=[scr2.buf])
        k.op("act", lambda h: h.activation(
            out=A(3), in_=A(1), func=AF.Exp, scale=-1.0),
            reads=[scr2.buf], writes=[scr2.buf])
        k.op("dve", lambda h, o=o: h.tensor_tensor(
            out=A(3), in0=A(3), in1=bob.ap(c0=o, cols=NSL), op=ALU.mult),
            reads=[scr2.buf, bob.buf], writes=[scr2.buf])
        k.op("dve", lambda h: h.memset(A(4, 1), 0.0), writes=[scr2.buf])
        k.op("dve", lambda h: h.tensor_copy(
            out=scr2.ap(c0=4 * NSL + 1, cols=NSL - 1), in_=A(3, NSL - 1)),
            reads=[scr2.buf], writes=[scr2.buf])
        k.op("dve", lambda h: h.tensor_tensor_scan(
            out=A(5), data0=c["onesf"].ap(cols=NSL), data1=A(4),
            initial=0.0, op0=ALU.mult, op1=ALU.add),
            reads=[c["onesf"].buf, scr2.buf], writes=[scr2.buf])
        k.op("act", lambda h: h.activation(out=A(3), in_=A(1), func=AF.Exp),
             reads=[scr2.buf], writes=[scr2.buf])
        k.op("dve", lambda h, jt=jt: h.tensor_tensor(
            out=carries.ap(c0=jt * NSL, cols=NSL), in0=A(5), in1=A(3),
            op=ALU.mult), reads=[scr2.buf], writes=[carries.buf])
    scr2.free()

    bc = bcast_rows(k, st, c, [st["Arow"], st["Brow"]], "fbc")
    for dti in range(DT):
        for n in range(2):
            k.op("dve", lambda h, dti=dti, n=n: h.tensor_tensor(
                out=xres.ap(c0=dti * S + n * 512, cols=512),
                in0=xres.ap(c0=dti * S + n * 512, cols=512),
                in1=bc[0][0](n), op=ALU.mult),
                reads=[xres.buf] + bc[0][1], writes=[xres.buf])
            k.op("dve", lambda h, dti=dti, n=n: h.tensor_tensor(
                out=xres.ap(c0=dti * S + n * 512, cols=512),
                in0=xres.ap(c0=dti * S + n * 512, cols=512),
                in1=bc[1][0](n), op=ALU.add),
                reads=[xres.buf] + bc[1][1], writes=[xres.buf])
    st["Arow"].free()
    st["Brow"].free()

    ub = bfa.alloc(DT * S, "ub")
    finb = bfa.alloc(S, "finb")
    mcl = fpa.alloc(S, "mcl")
    col = fpa.alloc(S, "col")
    gl = fpa.alloc(S, "gl")
    expl = fpa.alloc(S, "expl")
    for jt in range(DT):
        k.dma("sync", mcl.ap(), scr["mc"].ap()[jt], reads=[scrbuf["mc"]],
              writes=[mcl.buf], ch="mcl")
        k.dma("sync", col.ap(), scr["co"].ap()[jt], reads=[scrbuf["co"]],
              writes=[col.buf], ch="col")
        k.dma("sync", gl.ap(), scr["g"].ap()[jt], reads=[scrbuf["g"]],
              writes=[gl.buf], ch="gl")
        k.op("act", lambda h: h.activation(out=expl.ap(), in_=mcl.ap(),
                                           func=AF.Exp, scale=-1.0),
             reads=[mcl.buf], writes=[expl.buf])
        for cc in range(4):
            k.op("dve", lambda h, jt=jt, cc=cc: h.scalar_tensor_tensor(
                out=finb.ap(c0=cc * 256, cols=256),
                in0=expl.ap(c0=cc * 256, cols=256),
                scalar=carries.ap(c0=jt * NSL + 12 + cc, cols=1),
                in1=col.ap(c0=cc * 256, cols=256),
                op0=ALU.mult, op1=ALU.add),
                reads=[expl.buf, carries.buf, col.buf], writes=[finb.buf])
        for n in range(2):
            (po, pob) = st["pb"][n]
            k.op("pe", lambda h, n=n, po=po: h.matmul(
                po[:, 0:512], c["bdmix"].ap(),
                finb.ap(c0=n * 512, cols=512), start=True, stop=True),
                reads=[c["bdmix"].buf, finb.buf], writes=[pob])
            k.op("dve", lambda h, jt=jt, n=n, po=po: h.tensor_tensor(
                out=ub.ap(c0=jt * S + n * 512, cols=512), in0=po[:, 0:512],
                in1=gl.ap(c0=n * 512, cols=512), op=ALU.mult),
                reads=[pob, gl.buf], writes=[ub.buf])
    finb.free(); mcl.free(); col.free(); gl.free(); expl.free()
    carries.free()

    wob = bfa.alloc(DT * 128, "wob")
    wof = fpa.alloc(DT * 128, "wof")
    for dm in range(DT):
        k.dma("sync", wof.ap(), io["woT"][:, dm * DT * 128:(dm + 1) * DT * 128],
              writes=[wof.buf], ch="wol")
        k.op("act", lambda h: h.activation(out=wob.ap(), in_=wof.ap(),
                                           func=AF.Copy),
             reads=[wof.buf], writes=[wob.buf])
        for n in range(2):
            (po, pob) = st["pb"][2 + n]

            def emit(h, n=n, po=po):
                last = None
                for dti in range(DT):
                    last = h.matmul(po[:, 0:512],
                                    wob.ap(c0=dti * 128, cols=128),
                                    ub.ap(c0=dti * S + n * 512, cols=512),
                                    start=(dti == 0), stop=(dti == DT - 1))
                return last
            k.op("pe", emit, reads=[wob.buf, ub.buf], writes=[pob])
            k.op("dve", lambda h, dm=dm, n=n, po=po: h.tensor_tensor(
                out=xres.ap(c0=dm * S + n * 512, cols=512),
                in0=xres.ap(c0=dm * S + n * 512, cols=512),
                in1=po[:, 0:512], op=ALU.add),
                reads=[pob, xres.buf], writes=[xres.buf])
    wob.free(); wof.free(); ub.free()


def build_body(k, st):
    io = st["io"]
    c = load_consts(k, st)
    acc1 = wsum_stream(k, st, c, io["w1"], "sc1")
    sc1 = wsum_finish(k, st, c, acc1, "sc1")
    acc2 = wsum_stream(k, st, c, io["w2t"], "sc2")
    sc2 = st["sm"].alloc(4, "sc2")

    def sc2_fix():
        sc2n = wsum_finish(k, st, c, acc2, "sc2n")
        k.op("dve", lambda h, sc2n=sc2n: h.tensor_scalar(
            out=sc2n.ap(c0=3, cols=1), in0=sc2n.ap(cols=1), scalar1=0.5,
            scalar2=None, op0=ALU.mult), reads=[sc2n.buf], writes=[sc2n.buf])
        sc2.arena, sc2.off, sc2.cols, sc2.buf, sc2.req = \
            sc2n.arena, sc2n.off, sc2n.cols, sc2n.buf, sc2n.req
    st["sc2_fix"] = sc2_fix
    st["consts"] = c
    st["sc1"] = sc1
    st["sc2"] = sc2

    xres = st["xres"]
    nstrip = 1 if st["upto"] == "ffn1strip" else NSTRIP
    for strip in range(nstrip):
        k.dma("sync", xres.ap(),
              io["x"][:, strip * DT * S:(strip + 1) * DT * S],
              writes=[xres.buf], ch="xload")
        ffn_strip(k, st, c, sc1, sc2, strip, jit=(strip == 0))
        if st["upto"] in ("full", "attn"):
            attn_strip(k, st, c, strip, own=(strip == NSTRIP - 1))

    if st["upto"] in ("full", "attn"):
        carry_and_finish(k, st, c)
    if st["upto"] == "full":
        ffn_strip(k, st, c, sc1, sc2, NSTRIP, jit=False)

    k.dma("sync", io["out"], xres.ap(),
          reads=[xres.buf], writes=[Buf("out")], ch="out")
    k.wait_all_dma("gpsimd")



def kernel(**inputs):
    per_core = prep_inputs(**inputs)
    nc, st = build_program()
    res = run_bass_kernel_spmd(nc, per_core, core_ids=list(range(NCORE)))
    out = np.zeros((2, 4096, D), np.float32)
    for core in range(NCORE):
        b, q = core // 4, core % 4
        o = res.results[core]["out"].reshape(128, DT, S).transpose(1, 0, 2)
        out[b, S * q:S * (q + 1)] = o.reshape(D, S).T
    return out
